# version 03
# baseline (speedup 1.0000x reference)
import math
import numpy as np
import ml_dtypes

import concourse.bass as bass
import concourse.mybir as mybir
from concourse.bass_utils import run_bass_kernel_spmd

F32 = mybir.dt.float32
BF16 = mybir.dt.bfloat16
I32 = mybir.dt.int32
AF = mybir.ActivationFunctionType
ALU = mybir.AluOpType
AX = mybir.AxisListType
BF = ml_dtypes.bfloat16

NCORES = 8
B = 16
S = 2
T = 2048
C = 1024
H = 16
D = 64
NCH = C // 128
NT = T // 128
NTB = T // 512
NP = H // 2
MAGIC = float(1 << 23)
NEGBIG = -1.0e30
WARMUP = 1024

_CACHE = {}


def _host_consts():
    ident = np.eye(128, dtype=BF)
    qq = np.arange(128)[:, None]
    kk = np.arange(128)[None, :]
    maskbig = np.where(kk <= qq, 0.0, NEGBIG).astype(BF)
    ones6 = np.zeros((128, 128), np.float32)
    ones6[0:3, :] = 1.0
    ones6[64:67, :] = 1.0
    ones6 = ones6.astype(BF)
    start = 2.0 ** (-(2.0 ** (-(math.log2(H) - 3))))
    slopes = np.array([start * start**i for i in range(H)], np.float64)
    k = np.arange(T, dtype=np.float64)
    ramp3 = np.zeros((H, 3, T), BF)
    for h in range(H):
        v = slopes[h] * k
        hi = v.astype(BF)
        mid = (v - hi.astype(np.float64)).astype(BF)
        lo = (v - hi.astype(np.float64) - mid.astype(np.float64)).astype(BF)
        ramp3[h, 0], ramp3[h, 1], ramp3[h, 2] = hi, mid, lo
    return ident, maskbig, ones6, ramp3


class Ch:
    def __init__(self, nc, stack, name):
        self.sem = stack.enter_context(nc.semaphore(name))
        self.n = 0


class Sched:
    ENGS = ("tensor", "vector", "scalar", "sync", "gpsimd")

    def __init__(self):
        self.ops = {k: [] for k in self.ENGS}

    def do(self, eng, fn, inc=None, k=1):
        if inc is not None:
            inc.n += k
            self.ops[eng].append(
                lambda e, f=fn, s=inc.sem, kk=k: f(e).then_inc(s, kk))
        else:
            self.ops[eng].append(fn)

    def nop(self, eng):
        self.ops[eng].append(lambda e: e.nop())

    def wait(self, eng, ch, n=None):
        tgt = ch.n if n is None else n
        if tgt > 0:
            self.ops[eng].append(lambda e, s=ch.sem, t=tgt: e.wait_ge(s, t))

    def emit(self, nc, name=None):
        with nc.Block(name) as blk:
            for eng in self.ENGS:
                ops = self.ops[eng]
                if not ops:
                    continue
                getattr(blk, eng)(lambda e, o=tuple(ops): [f(e) for f in o])


class DStream:

    def __init__(self, ch, name):
        self.c = [ch(name + "a"), ch(name + "b")]

    def do(self, sc, eng, fn, idx):
        sc.do(eng, fn, inc=self.c[idx % 2], k=16)

    def wait_done(self, sc, eng, idx):
        if idx < 0:
            return
        sc.wait(eng, self.c[idx % 2], 16 * (idx // 2 + 1))

    def wait_all(self, sc, eng):
        sc.wait(eng, self.c[0])
        sc.wait(eng, self.c[1])


def r3(ap, b, f=128):
    return ap.rearrange("p (b f) -> p b f", b=b, f=f)


def build_program(dbg=(), detect_races=False):
    nc = bass.Bass("TRN2", target_bir_lowering=False,
                   detect_race_conditions=detect_races)

    x_in = nc.dram_tensor("x", [S, T, C], F32, kind="ExternalInput").ap()
    wa_in = nc.dram_tensor("wa", [C, 3 * C], F32, kind="ExternalInput").ap()
    wp_in = nc.dram_tensor("wp", [C, C], F32, kind="ExternalInput").ap()
    tw_in = nc.dram_tensor("tw", [C, C], I32, kind="ExternalInput").ap()
    ba24_in = nc.dram_tensor("ba24", [128, 24], F32, kind="ExternalInput").ap()
    bp8_in = nc.dram_tensor("bp8", [128, 8], F32, kind="ExternalInput").ap()
    bf8_in = nc.dram_tensor("bf8", [128, 8], F32, kind="ExternalInput").ap()
    bvr_in = nc.dram_tensor("bvrep", [128, C], F32, kind="ExternalInput").ap()
    sc_in = nc.dram_tensor("scals", [128, 2], F32, kind="ExternalInput").ap()
    id_in = nc.dram_tensor("cident", [128, 128], BF16, kind="ExternalInput").ap()
    mk_in = nc.dram_tensor("cmask", [128, 128], BF16, kind="ExternalInput").ap()
    o6_in = nc.dram_tensor("cones6", [128, 128], BF16, kind="ExternalInput").ap()
    rp_in = nc.dram_tensor("cramp3", [H, 3, T], BF16, kind="ExternalInput").ap()

    y_out = nc.dram_tensor("y", [S, T, C], F32, kind="ExternalOutput").ap()
    dbg_t = {}
    for name, shape, dt in (
        ("dQT", [128, NCH * T], BF16), ("dKT", [128, NCH * T], BF16),
        ("dVT", [128, NT * C], BF16), ("dHT", [128, NCH * T], BF16),
        ("dPR", [128, T], BF16), ("dX2", [128, 4 * C], F32),
        ("dWB", [128, NCH * 1536], BF16), ("dSC", [128, 24], F32),
        ("dXS", [128, 2 * T], F32), ("dD8", [128, 24], F32),
        ("dPB", [128, 4 * T], BF16), ("dPT", [128, NT * 512], BF16),
    ):
        if name in dbg:
            dbg_t[name] = nc.dram_tensor(name, shape, dt, kind="ExternalOutput").ap()

    ytk = "ExternalOutput" if "dYT" in dbg else "Internal"
    yt_dram = nc.dram_tensor("dYT", [NCH, 128, T], BF16, kind=ytk).ap()
    wp_dram = nc.dram_tensor("wpd", [C, C], BF16, kind="Internal").ap()
    wf_dram = nc.dram_tensor("wfd", [C, C], BF16, kind="Internal").ap()

    import contextlib
    with contextlib.ExitStack() as ST:
        E = ST.enter_context
        ident = E(nc.sbuf_tensor("identt", [128, 128], BF16))
        maskb = E(nc.sbuf_tensor("maskbt", [128, 128], BF16))
        ones6 = E(nc.sbuf_tensor("ones6t", [128, 128], BF16))
        ba24 = E(nc.sbuf_tensor("ba24t", [128, 24], F32))
        bp8 = E(nc.sbuf_tensor("bp8t", [128, 8], F32))
        bf8 = E(nc.sbuf_tensor("bf8t", [128, 8], F32))
        bqc8 = E(nc.sbuf_tensor("bqc8", [128, 8], F32))
        sc11 = E(nc.sbuf_tensor("sc11", [128, 8], F32))
        cscale = E(nc.sbuf_tensor("cscale", [128, 1], F32))
        epsb = E(nc.sbuf_tensor("epsb", [128, 1], F32))
        rsb = E(nc.sbuf_tensor("rsb", [128, 1], F32))
        ssb = E(nc.sbuf_tensor("ssb", [128, 1], F32))
        bvb = E(nc.sbuf_tensor("bvb", [128, C], F32))

        chans = {}

        def ch(name):
            if name not in chans:
                chans[name] = Ch(nc, ST, name)
            return chans[name]

        _phase_0(nc, ch, sc_in, id_in, mk_in, o6_in, ba24_in, bp8_in, bf8_in,
                 bvr_in, wp_in, tw_in, wp_dram, wf_dram, ident, maskb, ones6,
                 ba24, bp8, bf8, bqc8, sc11, cscale, rsb, ssb, bvb, epsb)
        for s in range(S):
            with (
                nc.sbuf_tensor(f"QT{s}", [128, NCH * T], BF16) as QT,
                nc.sbuf_tensor(f"KT{s}", [128, NCH * T], BF16) as KT,
                nc.sbuf_tensor(f"VT{s}", [128, NT * C], BF16) as VT,
            ):
                _phase_AB(nc, ch, s, x_in, wa_in, QT, KT, VT, ba24, bqc8,
                          cscale, bvb, epsb, dbg_t, sc11, ssb, rsb)
                _phase_C(nc, ch, s, QT, KT, VT, ident, maskb, ones6, rp_in,
                         yt_dram, dbg_t, epsb)
            _phase_D(nc, ch, s, x_in, y_out, yt_dram, wp_dram, wf_dram,
                     bp8, bf8, rsb, ssb, epsb, y_final=(s == S - 1), dbg_t=dbg_t)

    return nc


def _phase_0(nc, ch, sc_in, id_in, mk_in, o6_in, ba24_in, bp8_in, bf8_in,
             bvr_in, wp_in, tw_in, wp_dram, wf_dram, ident, maskb, ones6,
             ba24, bp8, bf8, bqc8, sc11, cscale, rsb, ssb, bvb, epsb):
    with (
        nc.sbuf_tensor("wstg_f", [128, 2 * C], F32) as wstg_f,
        nc.sbuf_tensor("wstg_i", [128, 2 * C], I32) as wstg_i,
        nc.sbuf_tensor("wstg_b", [128, 2 * C], BF16) as wstg_b,
        nc.sbuf_tensor("wstg_t", [128, 2 * C], BF16) as wstg_t,
        nc.sbuf_tensor("warm", [128, WARMUP], F32) as warm,
    ):
        sc = Sched()
        c_ld = ch("c_ld")
        sc.do("gpsimd", lambda e: e.memset(epsb[:, :], 1e-5), inc=ch("c_eps"))
        sc.wait("scalar", ch("c_eps"))
        for i in range(WARMUP):
            sc.do("scalar", lambda e, i=i: e.activation(
                out=warm[:, i:i + 1], in_=epsb[:, 0:1], func=AF.Exp,
                bias=epsb[:, 0:1], scale=1.0))
        for dst, src in ((ident, id_in), (maskb, mk_in), (ones6, o6_in),
                         (ba24, ba24_in), (bp8, bp8_in), (bf8, bf8_in),
                         (bvb, bvr_in)):
            sc.do("gpsimd", lambda e, d=dst, s=src: e.dma_start(
                out=d[:, :], in_=s[:, :]), inc=c_ld, k=16)
        sc.do("gpsimd", lambda e: e.dma_start(
            out=sc11[:, 0:2], in_=sc_in[:, :]), inc=c_ld, k=16)
        c_sc = ch("c_sc")
        c_a0 = ch("c_a0")
        sc.wait("vector", c_ld)
        sc.do("vector", lambda e: e.tensor_scalar(
            out=sc11[:, 2:3], in0=sc11[:, 0:1], scalar1=8.0, scalar2=None,
            op0=ALU.mult))
        sc.do("vector", lambda e: e.tensor_scalar(
            out=ssb[:, 0:1], in0=sc11[:, 1:2], scalar1=1e-5, scalar2=None,
            op0=ALU.max))
        sc.do("vector", lambda e: e.tensor_tensor(
            out=sc11[:, 3:4], in0=sc11[:, 2:3], in1=sc11[:, 2:3], op=ALU.mult))
        sc.do("vector", lambda e: e.tensor_scalar(
            out=sc11[:, 4:5], in0=sc11[:, 2:3], scalar1=1e38, scalar2=1.0,
            op0=ALU.mult, op1=ALU.min))
        sc.nop("vector")
        sc.do("vector", lambda e: e.tensor_scalar(
            out=sc11[:, 4:5], in0=sc11[:, 4:5], scalar1=-1.0, scalar2=None,
            op0=ALU.max), inc=c_sc)
        sc.wait("scalar", c_sc)
        sc.do("scalar", lambda e: e.activation(
            out=sc11[:, 5:6], in_=sc11[:, 3:4], func=AF.Ln))
        sc.do("scalar", lambda e: e.activation(
            out=sc11[:, 6:7], in_=ssb[:, 0:1], func=AF.Ln))
        sc.do("scalar", lambda e: e.activation(
            out=sc11[:, 5:6], in_=sc11[:, 5:6], func=AF.Exp, scale=-0.5))
        sc.do("scalar", lambda e: e.activation(
            out=rsb[:, 0:1], in_=sc11[:, 6:7], func=AF.Exp, scale=-1.0),
            inc=c_a0)
        c_bqc = ch("c_bqc")
        sc.wait("vector", c_a0)
        sc.do("vector", lambda e: e.tensor_tensor(
            out=cscale[:, 0:1], in0=sc11[:, 5:6], in1=sc11[:, 4:5], op=ALU.mult))
        sc.nop("vector")
        sc.do("vector", lambda e: e.tensor_scalar(
            out=bqc8[:, :], in0=ba24[:, 0:8], scalar1=cscale[:, 0:1], scalar2=None,
            op0=ALU.mult), inc=c_bqc)

        c_l = ch("c_l")
        c_wc = ch("c_wc")
        c_wpw = ch("c_wpw")
        c_wt = ch("c_wt")
        c_wfw = ch("c_wfw")
        for cc in range(NCH):
            pg = (cc % 2) * C
            sc.do("gpsimd", lambda e, cc=cc, pg=pg: e.dma_start(
                out=wstg_f[:, pg:pg + C], in_=wp_in[cc * 128:(cc + 1) * 128, :]),
                inc=c_l, k=16)
            sc.wait("vector", c_l)
            if cc >= 2:
                sc.wait("vector", c_wpw, 16 * (cc - 1))
            sc.do("vector", lambda e, pg=pg: e.tensor_copy(
                wstg_b[:, pg:pg + C], wstg_f[:, pg:pg + C]), inc=c_wc)
            sc.wait("gpsimd", c_wc)
            sc.do("gpsimd", lambda e, cc=cc, pg=pg: e.dma_start(
                out=wp_dram[cc * 128:(cc + 1) * 128, :], in_=wstg_b[:, pg:pg + C]),
                inc=c_wpw, k=16)
        for oc in range(NCH):
            pg = (oc % 2) * C
            sc.do("gpsimd", lambda e, oc=oc, pg=pg: e.dma_start(
                out=wstg_i[:, pg:pg + C], in_=tw_in[oc * 128:(oc + 1) * 128, :]),
                inc=c_l, k=16)
            sc.wait("vector", c_l)
            sc.wait("vector", c_wpw)
            if oc >= 2:
                sc.wait("vector", c_wt, 16 * (oc - 1))
            sc.do("vector", lambda e, pg=pg: e.tensor_scalar(
                out=wstg_b[:, pg:pg + C], in0=wstg_i[:, pg:pg + C],
                scalar1=1.0, scalar2=None, op0=ALU.subtract), inc=c_wc)
            sc.wait("sync", c_wc)
            if oc >= 1:
                sc.wait("sync", c_wfw, 16 * 8 * oc)
            sc.do("sync", lambda e, pg=pg: e.dma_start_transpose(
                r3(wstg_t[:, pg:pg + C], 8), r3(wstg_b[:, pg:pg + C], 8)),
                inc=c_wt, k=16)
            sc.wait("gpsimd", c_wt)
            for ic in range(NCH):
                sc.do("gpsimd", lambda e, oc=oc, ic=ic, pg=pg: e.dma_start(
                    out=wf_dram[ic * 128:(ic + 1) * 128, oc * 128:(oc + 1) * 128],
                    in_=wstg_t[:, pg + ic * 128:pg + (ic + 1) * 128]),
                    inc=c_wfw, k=16)
        sc.emit(nc, "ph0")


def _phase_AB(nc, ch, s, x_in, wa_in, QT, KT, VT, ba24, bqc8, cscale, bvb,
              epsb, dbg_t, sc11=None, ssb=None, rsb=None):
    with (
        nc.sbuf_tensor(f"hT{s}", [128, NT * C], BF16) as hT,
        nc.sbuf_tensor(f"xa{s}", [128, 2 * C], F32) as xa,
        nc.sbuf_tensor(f"hb{s}", [128, 2 * C], BF16) as hb,
        nc.sbuf_tensor(f"sqscr{s}", [128, C], BF16) as sqscr,
        nc.sbuf_tensor(f"lnst{s}", [128, 128], F32) as lnst,
        nc.sbuf_tensor(f"wab{s}", [128, NCH * 1536], BF16) as wab,
        nc.sbuf_tensor(f"wstg{s}", [128, 2 * 1536], F32) as wstg,
        nc.psum_tensor(f"pq{s}", [128, 1024], F32) as pq,
    ):
        sc = Sched()
        axs = DStream(ch, f"ax{s}")
        hts = DStream(ch, f"ht{s}")
        c_st = ch(f"c_st{s}")
        c_sq = ch(f"c_sq{s}")
        c_ap = ch(f"c_ap{s}")
        for tt in range(NT):
            lb = tt * 8
            pg = (tt % 2) * C
            if tt >= 2:
                sc.wait("sync", c_ap, tt - 1)
            axs.do(sc, "sync", lambda e, tt=tt, pg=pg: e.dma_start(
                out=xa[:, pg:pg + C], in_=x_in[s, tt * 128:(tt + 1) * 128, :]), tt)
            axs.wait_done(sc, "vector", tt)
            sc.do("vector", lambda e, pg=pg, lb=lb: e.reduce_sum(
                lnst[:, lb:lb + 1], xa[:, pg:pg + C], axis=AX.X))
            axs.wait_done(sc, "scalar", tt)
            sc.do("scalar", lambda e, pg=pg, lb=lb: e.activation(
                out=sqscr[:, :], in_=xa[:, pg:pg + C], func=AF.Square,
                accum_out=lnst[:, lb + 1:lb + 2]), inc=c_sq)
            sc.wait("vector", c_sq)
            sc.do("vector", lambda e, lb=lb: e.tensor_scalar(
                out=lnst[:, lb + 2:lb + 3], in0=lnst[:, lb:lb + 1],
                scalar1=1.0 / C, scalar2=None, op0=ALU.mult))
            sc.do("vector", lambda e, lb=lb: e.tensor_scalar(
                out=lnst[:, lb + 3:lb + 4], in0=lnst[:, lb + 1:lb + 2],
                scalar1=1.0 / C, scalar2=None, op0=ALU.mult))
            sc.do("vector", lambda e, lb=lb: e.tensor_tensor(
                out=lnst[:, lb + 4:lb + 5], in0=lnst[:, lb + 2:lb + 3],
                in1=lnst[:, lb + 2:lb + 3], op=ALU.mult))
            sc.nop("vector")
            sc.do("vector", lambda e, lb=lb: e.tensor_tensor(
                out=lnst[:, lb + 5:lb + 6], in0=lnst[:, lb + 3:lb + 4],
                in1=lnst[:, lb + 4:lb + 5], op=ALU.subtract), inc=c_st)
            sc.wait("scalar", c_st)
            sc.do("scalar", lambda e, lb=lb: e.activation(
                out=lnst[:, lb + 6:lb + 7], in_=lnst[:, lb + 5:lb + 6],
                func=AF.Ln, bias=epsb[:, 0:1], scale=1.0))
            sc.nop("scalar")
            sc.do("scalar", lambda e, lb=lb: e.activation(
                out=lnst[:, lb + 6:lb + 7], in_=lnst[:, lb + 6:lb + 7],
                func=AF.Exp, scale=-0.5), inc=c_sq)
            sc.wait("vector", c_sq)
            sc.do("vector", lambda e, lb=lb: e.scalar_tensor_tensor(
                out=lnst[:, lb + 7:lb + 8], in0=lnst[:, lb + 2:lb + 3],
                scalar=-1.0, in1=lnst[:, lb + 6:lb + 7],
                op0=ALU.mult, op1=ALU.mult), inc=c_st)
            sc.wait("scalar", c_st)
            hts.wait_done(sc, "scalar", tt - 2)
            sc.do("scalar", lambda e, pg=pg, lb=lb: e.activation(
                out=hb[:, pg:pg + C], in_=xa[:, pg:pg + C], func=AF.Identity,
                bias=lnst[:, lb + 7:lb + 8], scale=lnst[:, lb + 6:lb + 7]),
                inc=c_ap)
            sc.wait("sync", c_ap)
            hts.do(sc, "sync", lambda e, tt=tt, pg=pg: e.dma_start_transpose(
                r3(hT[:, tt * C:(tt + 1) * C], NCH),
                r3(hb[:, pg:pg + C], NCH)), tt)

        wss = DStream(ch, f"ws{s}")
        c_wc = ch(f"c_wc{s}")
        c_mm = ch(f"c_mm{s}")
        c_ev = ch(f"c_ev{s}")
        nmm = [0]
        nev = [0]
        hTv = hT[:, :].rearrange("p (t k) -> p t k", t=NT, k=C)
        for half in range(2):
            hoff = half * 1536
            for cc in range(NCH):
                gidx = half * NCH + cc
                pg = (gidx % 2) * 1536
                if gidx >= 2:
                    sc.wait("sync", c_wc, gidx - 1)
                if half == 1 and cc == 0:
                    sc.wait("vector", c_mm, 48)
                wss.do(sc, "sync", lambda e, cc=cc, pg=pg, hoff=hoff: e.dma_start(
                    out=wstg[:, pg:pg + 1536],
                    in_=wa_in[cc * 128:(cc + 1) * 128, hoff:hoff + 1536]), gidx)
                wss.wait_done(sc, "vector", gidx)
                sc.do("vector", lambda e, cc=cc, pg=pg: e.tensor_copy(
                    wab[:, cc * 1536:cc * 1536 + 1536], wstg[:, pg:pg + 1536]),
                    inc=c_wc)
            sc.wait("tensor", c_wc)
            hts.wait_all(sc, "tensor")
            for ocl in range(12):
                oc = half * 12 + ocl
                if oc >= 16:
                    break
                for tb in range(NTB):
                    bank = (nmm[0] % 2) * 512
                    if nmm[0] >= 2:
                        sc.wait("tensor", c_ev, nev[0] - 1)
                    for cc in range(NCH):
                        sc.do("tensor", lambda e, cc=cc, ocl=ocl, tb=tb, bank=bank:
                              e.matmul(
                                  pq[:, bank:bank + 512],
                                  wab[:, cc * 1536 + ocl * 128:cc * 1536 + ocl * 128 + 128],
                                  hTv[:, 4 * tb:4 * tb + 4, cc * 128:(cc + 1) * 128],
                                  start=(cc == 0), stop=(cc == NCH - 1)),
                              inc=(c_mm if cc == NCH - 1 else None))
                    nmm[0] += 1
                    sc.wait("scalar", c_mm)
                    if oc < 8:
                        sc.do("scalar", lambda e, oc=oc, tb=tb, bank=bank:
                              e.activation(
                                  out=QT[:, oc * T + tb * 512:oc * T + tb * 512 + 512],
                                  in_=pq[:, bank:bank + 512], func=AF.Identity,
                                  bias=bqc8[:, oc:oc + 1], scale=cscale[:, 0:1]),
                              inc=c_ev)
                    else:
                        sc.do("scalar", lambda e, oc=oc, tb=tb, bank=bank:
                              e.activation(
                                  out=KT[:, (oc - 8) * T + tb * 512:(oc - 8) * T + tb * 512 + 512],
                                  in_=pq[:, bank:bank + 512], func=AF.Identity,
                                  bias=ba24[:, oc:oc + 1], scale=1.0), inc=c_ev)
                    nev[0] += 1
        for tt in range(NT):
            for vb in range(2):
                bank = (nmm[0] % 2) * 512
                sc.wait("tensor", c_ev, nev[0] - 1)
                for cc in range(NCH):
                    sc.do("tensor", lambda e, cc=cc, tt=tt, vb=vb, bank=bank:
                          e.matmul(
                              pq[:, bank:bank + 512],
                              hT[:, tt * C + cc * 128:tt * C + cc * 128 + 128],
                              wab[:, cc * 1536 + 512 + vb * 512:cc * 1536 + 512 + vb * 512 + 512],
                              start=(cc == 0), stop=(cc == NCH - 1)),
                          inc=(c_mm if cc == NCH - 1 else None))
                nmm[0] += 1
                sc.wait("vector", c_mm)
                sc.do("vector", lambda e, tt=tt, vb=vb, bank=bank: e.tensor_tensor(
                    out=VT[:, tt * C + vb * 512:tt * C + vb * 512 + 512],
                    in0=pq[:, bank:bank + 512], in1=bvb[:, vb * 512:vb * 512 + 512],
                    op=ALU.add), inc=c_ev)
                nev[0] += 1

        if "dSC" in dbg_t and s == 0:
            sc.wait("gpsimd", c_ev, 1)
            sc.do("gpsimd", lambda e: e.dma_start(
                out=dbg_t["dSC"][:, 0:8], in_=sc11[:, :],
                allow_slow_non_contiguous=True), inc=ch("c_dbg"), k=16)
            sc.do("gpsimd", lambda e: e.dma_start(
                out=dbg_t["dSC"][:, 8:9], in_=cscale[:, :],
                allow_slow_non_contiguous=True), inc=ch("c_dbg"), k=16)
            sc.do("gpsimd", lambda e: e.dma_start(
                out=dbg_t["dSC"][:, 9:10], in_=ssb[:, :],
                allow_slow_non_contiguous=True), inc=ch("c_dbg"), k=16)
            sc.do("gpsimd", lambda e: e.dma_start(
                out=dbg_t["dSC"][:, 10:11], in_=rsb[:, :],
                allow_slow_non_contiguous=True), inc=ch("c_dbg"), k=16)
            sc.do("gpsimd", lambda e: e.dma_start(
                out=dbg_t["dSC"][:, 11:19], in_=bqc8[:, :],
                allow_slow_non_contiguous=True), inc=ch("c_dbg"), k=16)
        if "dHT" in dbg_t and s == 0:
            hts.wait_all(sc, "gpsimd")
            sc.do("gpsimd", lambda e: e.dma_start(out=dbg_t["dHT"][:, :], in_=hT[:, :]),
                  inc=ch("c_dbg"), k=16)
        if "dWB" in dbg_t and s == 0:
            sc.wait("gpsimd", c_wc)
            sc.do("gpsimd", lambda e: e.dma_start(out=dbg_t["dWB"][:, :], in_=wab[:, :]),
                  inc=ch("c_dbg"), k=16)
        if ("dQT" in dbg_t or "dVT" in dbg_t) and s == 0:
            sc.wait("gpsimd", c_ev)
            for nm, t in (("dQT", QT), ("dKT", KT), ("dVT", VT)):
                if nm in dbg_t:
                    sc.do("gpsimd", lambda e, nm=nm, t=t: e.dma_start(
                        out=dbg_t[nm][:, :], in_=t[:, :]), inc=ch("c_dbg"), k=16)
        sc.emit(nc, f"phAB{s}")


def _phase_C(nc, ch, s, QT, KT, VT, ident, maskb, ones6, rp_in, yt_dram,
             dbg_t, epsb=None):
    with (
        nc.sbuf_tensor(f"rampt{s}", [67, 2 * T], BF16) as rampt,
        nc.sbuf_tensor(f"Xs{s}", [128, 2 * T], F32) as Xs,
        nc.sbuf_tensor(f"Pb0{s}", [128, 4 * T], BF16) as Pb0,
        nc.sbuf_tensor(f"Pb1{s}", [128, 4 * T], BF16) as Pb1,
        nc.sbuf_tensor(f"Pn{s}", [128, 2 * T], BF16) as Pn,
        nc.sbuf_tensor(f"PT0{s}", [128, NT * 512], BF16) as PT0,
        nc.sbuf_tensor(f"PT1{s}", [128, NT * 512], BF16) as PT1,
        nc.sbuf_tensor(f"negp{s}", [128, 8], F32) as negp,
        nc.sbuf_tensor(f"den8{s}", [128, 8], F32) as den8,
        nc.sbuf_tensor(f"r8{s}", [128, 8], F32) as r8,
        nc.sbuf_tensor(f"yts{s}", [128, 2 * 512], BF16) as yts,
        nc.psum_tensor(f"SH0{s}", [128, 1536], F32) as SH0,
        nc.psum_tensor(f"SH1{s}", [128, 1536], F32) as SH1,
        nc.psum_tensor(f"AVB{s}", [128, 1024], F32) as AVB,
    ):
        sc = Sched()
        c_rl = ch(f"c_rl{s}")
        c_sm = ch(f"c_sm{s}")
        c_se = ch(f"c_se{s}")
        c_xr = ch(f"c_xr{s}")
        c_ex = ch(f"c_ex{s}")
        c_rr = ch(f"c_rr{s}")
        c_pn = ch(f"c_pn{s}")
        pns = DStream(ch, f"pt{s}")
        c_am = ch(f"c_am{s}")
        c_ye = ch(f"c_ye{s}")
        yss = DStream(ch, f"ys{s}")
        Pb = (Pb0, Pb1)
        ptflat = (PT0, PT1)
        ptview = [PT0[:, :].rearrange("p (qs k) -> p qs k", qs=4, k=NT * 128),
                  PT1[:, :].rearrange("p (qs k) -> p qs k", qs=4, k=NT * 128)]
        xbuf_last_exp = [0, 0]
        se_last = {0: 0, 1: 0}
        gidx = [0]
        nyts = [0]
        npn = [0]

        for p in range(NP):
            rpg = (p % 2) * T
            if p >= 2:
                sc.wait("gpsimd", c_sm, c_sm.n)
            sc.do("gpsimd", lambda e, p=p, rpg=rpg: e.dma_start(
                out=rampt[0:3, rpg:rpg + T], in_=rp_in[2 * p, :, :]), inc=c_rl, k=16)
            sc.do("gpsimd", lambda e, p=p, rpg=rpg: e.dma_start(
                out=rampt[64:67, rpg:rpg + T], in_=rp_in[2 * p + 1, :, :]),
                inc=c_rl, k=16)
            sc.wait("tensor", c_rl)
            for qb in range(NTB):
                g = p * NTB + qb
                for qsub in range(4):
                    qt = qb * 4 + qsub
                    ktot = (qt + 1) * 128
                    qoff = qt * 128
                    chunks = [(0, min(ktot, 1536))]
                    if ktot > 1536:
                        chunks.append((1536, ktot - 1536))
                    for h in range(2):
                        SH = SH0 if h == 0 else SH1
                        row0 = 0 if h == 0 else 64
                        tp = (row0, 0)
                        Qst = QT[row0:row0 + 64, p * T + qoff:p * T + qoff + 128]
                        for ci, (c0, w) in enumerate(chunks):
                            sc.wait("tensor", c_se, se_last[h])
                            kbs = [(c0 + i * 512, min(512, w - i * 512))
                                   for i in range((w + 511) // 512)]
                            last_kb = len(kbs) - 1
                            has_diag = (c0 + w == ktot)
                            for i, (k0, kw) in enumerate(kbs):
                                sc.do("tensor", lambda e, SH=SH, c0=c0, k0=k0,
                                      kw=kw, Qst=Qst, row0=row0, p=p, tp=tp:
                                      e.matmul(
                                          SH[:, k0 - c0:k0 - c0 + kw], Qst,
                                          KT[row0:row0 + 64,
                                             p * T + k0:p * T + k0 + kw],
                                          start=True, stop=False,
                                          tile_position=tp))
                            for i, (k0, kw) in enumerate(kbs):
                                isdiagkb = has_diag and i == last_kb
                                stop = not isdiagkb
                                fin = (i == last_kb) and not has_diag
                                sc.do("tensor", lambda e, SH=SH, c0=c0, k0=k0,
                                      kw=kw, row0=row0, rpg=rpg, stop=stop, tp=tp:
                                      e.matmul(
                                          SH[:, k0 - c0:k0 - c0 + kw],
                                          ones6[row0:row0 + 3, :],
                                          rampt[row0:row0 + 3,
                                                rpg + k0:rpg + k0 + kw],
                                          start=False, stop=stop,
                                          tile_position=tp),
                                      inc=(c_sm if fin else None))
                            if has_diag:
                                sc.do("tensor", lambda e, SH=SH, c0=c0, qoff=qoff:
                                      e.matmul(
                                          SH[:, qoff - c0:qoff - c0 + 128],
                                          ident[:, :], maskb[:, :],
                                          start=False, stop=True), inc=c_sm)
                            b = gidx[0] % 2
                            sc.wait("vector", c_sm)
                            if ci == 0:
                                sc.wait("vector", c_ex, xbuf_last_exp[b])
                            last_ev = (ci == len(chunks) - 1)
                            sc.do("vector", lambda e, SH=SH, b=b, c0=c0, w=w,
                                  h=h, ci=ci: e.tensor_scalar(
                                      out=Xs[:, b * T + c0:b * T + c0 + w],
                                      in0=SH[:, 0:w], scalar1=-1.0, scalar2=None,
                                      op0=ALU.mult, op1=ALU.min,
                                      accum_out=negp[:, h * 4 + ci:h * 4 + ci + 1]),
                                  inc=c_se)
                            se_last[h] = c_se.n
                        nch_ = len(chunks)
                        if nch_ == 1:
                            negm_ap = negp[:, h * 4:h * 4 + 1]
                            sc.do("vector", lambda e, h=h: e.tensor_scalar(
                                out=negp[:, h * 4 + 2:h * 4 + 3],
                                in0=epsb[:, 0:1], scalar1=1.0,
                                scalar2=None, op0=ALU.mult), inc=c_xr)
                        else:
                            negm_ap = negp[:, h * 4 + 3:h * 4 + 4]
                            sc.do("vector", lambda e, h=h: e.tensor_scalar(
                                out=negp[:, h * 4 + 2:h * 4 + 3],
                                in0=epsb[:, 0:1], scalar1=1.0,
                                scalar2=None, op0=ALU.mult))
                            sc.do("vector", lambda e, h=h, nch_=nch_:
                                  e.tensor_reduce(
                                      negp[:, h * 4 + 3:h * 4 + 4],
                                      negp[:, h * 4:h * 4 + nch_],
                                      axis=AX.X, op=ALU.min), inc=c_xr)
                        b = gidx[0] % 2
                        sc.wait("scalar", c_xr)
                        sc.wait("scalar", c_pn, 8 * g)
                        sc.do("scalar", lambda e, b=b, h=h, qsub=qsub, ktot=ktot,
                              negm_ap=negm_ap: e.activation(
                                  out=Pb[h][:, qsub * T:qsub * T + ktot],
                                  in_=Xs[:, b * T:b * T + ktot], func=AF.Exp,
                                  bias=negm_ap, scale=-1.0,
                                  accum_out=den8[:, h * 4 + qsub:h * 4 + qsub + 1]),
                              inc=c_ex)
                        xbuf_last_exp[b] = c_ex.n
                        gidx[0] += 1
                        if ("dXS" in dbg_t and s == 0 and p == 0 and qb == 0
                                and qsub == 1 and h == 0):
                            cdbg = ch("c_dbg")
                            sc.wait("gpsimd", c_ex)
                            sc.do("gpsimd", lambda e: e.dma_start(
                                out=dbg_t["dXS"][:, :], in_=Xs[:, :]),
                                inc=cdbg, k=16)
                            sc.do("gpsimd", lambda e: e.dma_start(
                                out=dbg_t["dD8"][:, 0:8], in_=negp[:, :],
                                allow_slow_non_contiguous=True), inc=cdbg, k=16)
                            sc.wait("vector", cdbg)
                            sc.wait("scalar", cdbg)
                            sc.wait("tensor", cdbg)
                sc.wait("scalar", c_pn, 8 * g)
                sc.do("scalar", lambda e: e.activation(
                    out=negp[:, 2:3], in_=epsb[:, 0:1], func=AF.Identity,
                    bias=0.0, scale=1.0))
                sc.do("scalar", lambda e: e.activation(
                    out=r8[:, :], in_=den8[:, :], func=AF.Ln))
                sc.nop("scalar")
                sc.do("scalar", lambda e: e.activation(
                    out=r8[:, :], in_=r8[:, :], func=AF.Exp, scale=-1.0),
                    inc=c_rr)
                if "dPB" in dbg_t and "dD8" in dbg_t and s == 0 and p == 0 and qb == 0:
                    cdbg = ch("c_dbg")
                    sc.wait("gpsimd", c_rr)
                    sc.do("gpsimd", lambda e: e.dma_start(
                        out=dbg_t["dD8"][:, 8:16], in_=den8[:, :],
                        allow_slow_non_contiguous=True), inc=cdbg, k=16)
                    sc.do("gpsimd", lambda e: e.dma_start(
                        out=dbg_t["dD8"][:, 16:24], in_=r8[:, :],
                        allow_slow_non_contiguous=True), inc=cdbg, k=16)
                    sc.do("gpsimd", lambda e: e.dma_start(
                        out=dbg_t["dPB"][:, :], in_=Pb0[:, :]), inc=cdbg, k=16)
                    sc.wait("vector", cdbg)
                    sc.wait("scalar", cdbg)
                    sc.wait("tensor", cdbg)
                for h in range(2):
                    for qsub in range(4):
                        qt = qb * 4 + qsub
                        ktot = (qt + 1) * 128
                        i = npn[0]
                        png = (i % 2) * T
                        sc.wait("vector", c_rr)
                        pns.wait_done(sc, "vector", i - 1)
                        pns.wait_done(sc, "vector", i - 2)
                        sc.do("vector", lambda e, h=h, qsub=qsub, ktot=ktot,
                              png=png: e.tensor_scalar(
                                  out=Pn[:, png:png + ktot],
                                  in0=Pb[h][:, qsub * T:qsub * T + ktot],
                                  scalar1=r8[:, h * 4 + qsub:h * 4 + qsub + 1],
                                  scalar2=None, op0=ALU.mult), inc=c_pn)
                        eng = "sync" if (i % 2 == 0) else "scalar"
                        sc.wait(eng, c_pn)
                        sc.wait(eng, c_am, c_am.n)
                        pns.do(sc, eng, lambda e, h=h, qt=qt, qsub=qsub,
                               ktot=ktot, png=png: e.dma_start_transpose(
                                   r3(ptflat[h][:, qsub * (NT * 128):
                                                qsub * (NT * 128) + ktot], qt + 1),
                                   r3(Pn[:, png:png + ktot], qt + 1)), i)
                        npn[0] += 1
                if "dPT" in dbg_t and s == 0 and p == 0 and qb == 0:
                    cdbg = ch("c_dbg")
                    pns.wait_all(sc, "gpsimd")
                    sc.do("gpsimd", lambda e: e.dma_start(
                        out=dbg_t["dPT"][:, :], in_=PT0[:, :]), inc=cdbg, k=16)
                    sc.wait("vector", cdbg)
                    sc.wait("scalar", cdbg)
                    sc.wait("tensor", cdbg)
                    sc.wait("sync", cdbg)
                nkc = 4 * qb + 4
                pns.wait_all(sc, "tensor")
                sc.wait("tensor", c_ye, c_ye.n)
                for kc in range(nkc):
                    qsmin = max(0, kc - 4 * qb)
                    for h in range(2):
                        first = (kc == 0)
                        last = (kc == nkc - 1)
                        sc.do("tensor", lambda e, kc=kc, h=h, p=p,
                              qsmin=qsmin, first=first, last=last: e.matmul(
                                  AVB[h * 64:h * 64 + 64,
                                      h * 512 + qsmin * 128:h * 512 + 512],
                                  VT[:, kc * C + p * 128 + h * 64:
                                     kc * C + p * 128 + h * 64 + 64],
                                  ptview[h][:, qsmin:4, kc * 128:kc * 128 + 128],
                                  start=first, stop=last,
                                  tile_position=(0, h * 64)),
                              inc=(c_am if last else None))
                sc.wait("scalar", c_am)
                yss.wait_done(sc, "scalar", nyts[0] - 2)
                yb = (nyts[0] % 2) * 512
                sc.do("scalar", lambda e, yb=yb: e.copy(
                    out=yts[0:64, yb:yb + 512], in_=AVB[0:64, 0:512]))
                sc.do("scalar", lambda e, yb=yb: e.copy(
                    out=yts[64:128, yb:yb + 512], in_=AVB[64:128, 512:1024]),
                    inc=c_ye)
                sc.wait("gpsimd", c_ye)
                yss.do(sc, "gpsimd", lambda e, p=p, qb=qb, yb=yb: e.dma_start(
                    out=yt_dram[p, :, qb * 512:qb * 512 + 512],
                    in_=yts[:, yb:yb + 512]), nyts[0])
                nyts[0] += 1
            if "dPR" in dbg_t and s == 0 and p == 0:
                sc.wait("gpsimd", c_pn)
                sc.do("gpsimd", lambda e: e.dma_start(
                    out=dbg_t["dPR"][:, :], in_=Pn[:, 0:T]), inc=ch("c_dbg"), k=16)
        sc.emit(nc, f"phC{s}")


def _phase_D(nc, ch, s, x_in, y_out, yt_dram, wp_dram, wf_dram, bp8, bf8,
             rsb, ssb, epsb, y_final, dbg_t):
    with (
        nc.sbuf_tensor(f"Wp{s}", [128, NCH * C], BF16) as Wp,
        nc.sbuf_tensor(f"Wf{s}", [128, NCH * C], BF16) as Wf,
        nc.sbuf_tensor(f"ytl{s}", [128, 2 * NCH * 512], BF16) as ytl,
        nc.sbuf_tensor(f"aT{s}", [128, NCH * 512], BF16) as aT,
        nc.sbuf_tensor(f"h2T{s}", [128, NCH * 512], BF16) as h2T,
        nc.sbuf_tensor(f"atok{s}", [128, 4 * C], BF16) as atok,
        nc.sbuf_tensor(f"ftok{s}", [128, 4 * C], BF16) as ftok,
        nc.sbuf_tensor(f"xd{s}", [128, 2 * C], F32) as xd,
        nc.sbuf_tensor(f"x2{s}", [128, 4 * C], F32) as x2,
        nc.sbuf_tensor(f"h2s{s}", [128, C], F32) as h2s,
        nc.sbuf_tensor(f"h2q{s}", [128, 2 * C], BF16) as h2q,
        nc.sbuf_tensor(f"an{s}", [128, C], F32) as an,
        nc.sbuf_tensor(f"xo{s}", [128, 2 * C], F32) as xo,
        nc.sbuf_tensor(f"sqs2{s}", [128, C], BF16) as sqs2,
        nc.sbuf_tensor(f"ln2{s}", [128, 12], F32) as ln2,
        nc.psum_tensor(f"pd{s}", [128, 1024], F32) as pd,
    ):
        sc = Sched()
        c_wl = ch(f"c_wl{s}")
        for cc in range(NCH):
            sc.do("gpsimd", lambda e, cc=cc: e.dma_start(
                out=Wp[:, cc * C:(cc + 1) * C],
                in_=wp_dram[cc * 128:(cc + 1) * 128, :]), inc=c_wl, k=16)
            sc.do("gpsimd", lambda e, cc=cc: e.dma_start(
                out=Wf[:, cc * C:(cc + 1) * C],
                in_=wf_dram[cc * 128:(cc + 1) * 128, :]), inc=c_wl, k=16)
        c_yl = ch(f"c_yl{s}")
        c_dmm = ch(f"c_dmm{s}")
        c_dev = ch(f"c_dev{s}")
        c_at = ch(f"c_at{s}")
        c_au = ch(f"c_au{s}")
        c_qt = ch(f"c_qt{s}")
        xls = DStream(ch, f"xl{s}")
        c_rx = ch(f"c_rx{s}")
        c_ds = ch(f"c_ds{s}")
        c_da = ch(f"c_da{s}")
        c_r1 = ch(f"c_r1{s}")
        c_qz = ch(f"c_qz{s}")
        c_f2 = ch(f"c_f2{s}")
        c_fu = ch(f"c_fu{s}")
        sos = DStream(ch, f"so{s}")
        nmm = [0]
        atokv = atok[:, :].rearrange("p (oc k) -> p oc k", oc=NCH, k=512)
        ftokv = ftok[:, :].rearrange("p (oc k) -> p oc k", oc=NCH, k=512)

        def ln_stats(tile_ap, base, ax=AX.X):
            sc.do("vector", lambda e, t=tile_ap, bs=base, ax=ax: e.reduce_sum(
                ln2[:, bs:bs + 1], t, axis=ax))
            sc.do("scalar", lambda e, t=tile_ap, bs=base: e.activation(
                out=sqs2[:, :], in_=t, func=AF.Square,
                accum_out=ln2[:, bs + 1:bs + 2]), inc=c_da)
            sc.wait("vector", c_da)
            sc.do("vector", lambda e, bs=base: e.tensor_scalar(
                out=ln2[:, bs:bs + 1], in0=ln2[:, bs:bs + 1],
                scalar1=1.0 / C, scalar2=None, op0=ALU.mult))
            sc.do("vector", lambda e, bs=base: e.tensor_scalar(
                out=ln2[:, bs + 1:bs + 2], in0=ln2[:, bs + 1:bs + 2],
                scalar1=1.0 / C, scalar2=None, op0=ALU.mult))
            sc.do("vector", lambda e, bs=base: e.tensor_tensor(
                out=ln2[:, bs + 2:bs + 3], in0=ln2[:, bs:bs + 1],
                in1=ln2[:, bs:bs + 1], op=ALU.mult))
            sc.nop("vector")
            sc.do("vector", lambda e, bs=base: e.tensor_tensor(
                out=ln2[:, bs + 1:bs + 2], in0=ln2[:, bs + 1:bs + 2],
                in1=ln2[:, bs + 2:bs + 3], op=ALU.subtract), inc=c_ds)
            sc.wait("scalar", c_ds)
            sc.do("scalar", lambda e, bs=base: e.activation(
                out=ln2[:, bs + 1:bs + 2], in_=ln2[:, bs + 1:bs + 2],
                func=AF.Ln, bias=epsb[:, 0:1], scale=1.0))
            sc.nop("scalar")
            sc.do("scalar", lambda e, bs=base: e.activation(
                out=ln2[:, bs + 1:bs + 2], in_=ln2[:, bs + 1:bs + 2],
                func=AF.Exp, scale=-0.5), inc=c_da)
            sc.wait("vector", c_da)
            sc.do("vector", lambda e, bs=base: e.scalar_tensor_tensor(
                out=ln2[:, bs + 2:bs + 3], in0=ln2[:, bs:bs + 1], scalar=-1.0,
                in1=ln2[:, bs + 1:bs + 2], op0=ALU.mult, op1=ALU.mult), inc=c_ds)
            return ln2[:, base + 1:base + 2], ln2[:, base + 2:base + 3]

        for tb in range(NTB):
            ypg = (tb % 2) * NCH * 512
            for cc in range(NCH):
                if tb >= 2 and cc == 0:
                    sc.wait("sync", c_dmm, nmm[0] - 8)
                sc.do("sync", lambda e, cc=cc, tb=tb, ypg=ypg: e.dma_start(
                    out=ytl[:, ypg + cc * 512:ypg + cc * 512 + 512],
                    in_=yt_dram[cc, :, tb * 512:tb * 512 + 512]), inc=c_yl, k=16)
            sc.wait("tensor", c_yl)
            sc.wait("tensor", c_wl, 16 * 16)
            for oc in range(NCH):
                bank = (nmm[0] % 2) * 512
                if nmm[0] >= 2:
                    sc.wait("tensor", c_dev, c_dev.n - 1)
                for cc in range(NCH):
                    sc.do("tensor", lambda e, cc=cc, oc=oc, bank=bank, ypg=ypg:
                          e.matmul(pd[:, bank:bank + 512],
                                   Wp[:, cc * C + oc * 128:cc * C + oc * 128 + 128],
                                   ytl[:, ypg + cc * 512:ypg + cc * 512 + 512],
                                   start=(cc == 0), stop=(cc == NCH - 1)),
                          inc=(c_dmm if cc == NCH - 1 else None))
                nmm[0] += 1
                sc.wait("scalar", c_dmm)
                if tb >= 1 and oc == 0:
                    sc.wait("scalar", c_f2, 16 * 8 * tb)
                sc.do("scalar", lambda e, oc=oc, bank=bank: e.activation(
                    out=aT[:, oc * 512:oc * 512 + 512], in_=pd[:, bank:bank + 512],
                    func=AF.Identity, bias=bp8[:, oc:oc + 1], scale=1.0),
                    inc=c_dev)
            for oc in range(NCH):
                sc.wait("sync", c_dev)
                sc.wait("sync", c_au, 4 * tb)
                sc.do("sync", lambda e, oc=oc: e.dma_start_transpose(
                    r3(atok[:, oc * 512:oc * 512 + 512], 4),
                    r3(aT[:, oc * 512:oc * 512 + 512], 4)), inc=c_at, k=16)
            for tt in range(4):
                gt = tb * 4 + tt
                xpg = (tt % 2) * C
                sc.wait("sync", c_rx, gt - 1)
                xls.do(sc, "sync", lambda e, tb=tb, tt=tt, xpg=xpg: e.dma_start(
                    out=xd[:, xpg:xpg + C],
                    in_=x_in[s, tb * 512 + tt * 128:tb * 512 + tt * 128 + 128, :]),
                    gt)
                sc.wait("vector", c_at, 16 * 8 * (tb + 1))
                sc.wait("scalar", c_at, 16 * 8 * (tb + 1))
                at_ap = atokv[:, :, tt * 128:(tt + 1) * 128]
                rstd, negmr = ln_stats(at_ap, 0, ax=AX.XY)
                sc.wait("scalar", c_ds)
                sc.do("scalar", lambda e, at_ap=at_ap, r=rstd, nm=negmr:
                      e.activation(out=an[:, :], in_=at_ap, func=AF.Identity,
                                   bias=nm, scale=r), inc=c_au)
                sc.wait("vector", c_au)
                xls.wait_done(sc, "vector", gt)
                sc.do("vector", lambda e, tt=tt, xpg=xpg: e.tensor_tensor(
                    out=x2[:, tt * C:(tt + 1) * C], in0=xd[:, xpg:xpg + C],
                    in1=an[:, :], op=ALU.add), inc=c_rx)
                sc.nop("vector")
                sc.wait("scalar", c_rx)
                x2_ap = x2[:, tt * C:(tt + 1) * C]
                rstd2, negmr2 = ln_stats(x2_ap, 4)
                sc.do("vector", lambda e: e.tensor_tensor(
                    out=ln2[:, 8:9], in0=ln2[:, 5:6], in1=rsb[:, 0:1], op=ALU.mult))
                sc.do("vector", lambda e: e.tensor_tensor(
                    out=ln2[:, 9:10], in0=ln2[:, 6:7], in1=rsb[:, 0:1],
                    op=ALU.mult), inc=c_ds)
                sc.wait("scalar", c_ds)
                sc.wait("scalar", c_qz, c_qz.n)
                sc.do("scalar", lambda e, x2_ap=x2_ap: e.activation(
                    out=h2s[:, :], in_=x2_ap, func=AF.Identity,
                    bias=ln2[:, 9:10], scale=ln2[:, 8:9]), inc=c_da)
                sc.wait("vector", c_da)
                qpg = (tt % 2) * C
                sc.wait("vector", c_qt, c_qt.n)
                sc.do("vector", lambda e: e.tensor_scalar(
                    out=h2s[:, :], in0=h2s[:, :], scalar1=127.0, scalar2=MAGIC,
                    op0=ALU.min, op1=ALU.add))
                sc.nop("vector")
                sc.do("vector", lambda e, qpg=qpg: e.tensor_scalar(
                    out=h2q[:, qpg:qpg + C], in0=h2s[:, :],
                    scalar1=MAGIC - 127.0, scalar2=MAGIC,
                    op0=ALU.max, op1=ALU.subtract), inc=c_qz)
                sc.wait("scalar", c_qz)
                sc.do("scalar", lambda e, tt=tt, qpg=qpg: e.dma_start_transpose(
                    r3(h2T[:, tt * C:(tt + 1) * C], NCH),
                    r3(h2q[:, qpg:qpg + C], NCH)), inc=c_qt, k=16)
            sc.wait("tensor", c_qt)
            h2Tv = h2T[:, :].rearrange("p (t k) -> p t k", t=4, k=C)
            for oc in range(NCH):
                bank = (nmm[0] % 2) * 512
                sc.wait("tensor", c_dev, c_dev.n - 1)
                for ic in range(NCH):
                    sc.do("tensor", lambda e, ic=ic, oc=oc, bank=bank, h2Tv=h2Tv:
                          e.matmul(pd[:, bank:bank + 512],
                                   Wf[:, ic * C + oc * 128:ic * C + oc * 128 + 128],
                                   h2Tv[:, :, ic * 128:(ic + 1) * 128],
                                   start=(ic == 0), stop=(ic == NCH - 1)),
                          inc=(c_dmm if ic == NCH - 1 else None))
                nmm[0] += 1
                sc.wait("scalar", c_dmm)
                if oc == 0:
                    sc.wait("scalar", c_at, 16 * 8 * (tb + 1))
                sc.do("scalar", lambda e, oc=oc, bank=bank: e.activation(
                    out=aT[:, oc * 512:oc * 512 + 512], in_=pd[:, bank:bank + 512],
                    func=AF.Identity, bias=bf8[:, oc:oc + 1], scale=ssb[:, 0:1]),
                    inc=c_dev)
            for oc in range(NCH):
                sc.wait("sync", c_dev)
                sc.wait("sync", c_fu, 4 * tb)
                sc.do("sync", lambda e, oc=oc: e.dma_start_transpose(
                    r3(ftok[:, oc * 512:oc * 512 + 512], 4),
                    r3(aT[:, oc * 512:oc * 512 + 512], 4)), inc=c_f2, k=16)
            for tt in range(4):
                gt = tb * 4 + tt
                sc.wait("vector", c_f2, 16 * 8 * (tb + 1))
                sc.wait("scalar", c_f2, 16 * 8 * (tb + 1))
                f_ap = ftokv[:, :, tt * 128:(tt + 1) * 128]
                rstdf, negmrf = ln_stats(f_ap, 0, ax=AX.XY)
                sc.wait("scalar", c_ds)
                sc.do("scalar", lambda e, f_ap=f_ap, r=rstdf, nm=negmrf:
                      e.activation(out=an[:, :], in_=f_ap, func=AF.Identity,
                                   bias=nm, scale=r), inc=c_fu)
                sc.wait("vector", c_fu)
                opg = (tt % 2) * C
                sos.wait_done(sc, "vector", gt - 2)
                sc.do("vector", lambda e, tt=tt, opg=opg: e.tensor_tensor(
                    out=xo[:, opg:opg + C], in0=x2[:, tt * C:(tt + 1) * C],
                    in1=an[:, :], op=ALU.add), inc=c_r1)
                sc.wait("gpsimd", c_r1)
                sos.do(sc, "gpsimd", lambda e, tb=tb, tt=tt, opg=opg: e.dma_start(
                    out=y_out[s, tb * 512 + tt * 128:tb * 512 + tt * 128 + 128, :],
                    in_=xo[:, opg:opg + C]), gt)
            if "dX2" in dbg_t and s == 0 and tb == 0:
                sc.wait("gpsimd", c_rx)
                sc.do("gpsimd", lambda e: e.dma_start(
                    out=dbg_t["dX2"][:, :], in_=x2[:, :]), inc=ch("c_dbg"), k=16)
        if y_final:
            sos.wait_all(sc, "gpsimd")
            if s == 1:
                sc.wait("gpsimd", ch("so0a"))
                sc.wait("gpsimd", ch("so0b"))
        sc.emit(nc, f"phD{s}")


def _prep_inputs(x, wa, ba, wp, bp, tmp, tw, bff, asc):
    ident, maskbig, ones6, ramp3 = _host_consts()
    common = {
        "wa": wa, "wp": wp, "tw": tw,
        "ba24": np.ascontiguousarray(ba.reshape(24, 128).T),
        "bp8": np.ascontiguousarray(bp.reshape(8, 128).T),
        "bf8": np.ascontiguousarray(bff.reshape(8, 128).T),
        "bvrep": np.ascontiguousarray(
            np.broadcast_to(ba[2 * C:3 * C].reshape(1, C), (128, C))),
        "scals": np.ascontiguousarray(np.broadcast_to(
            np.array([[float(tmp[0]), float(asc[0])]], np.float32), (128, 2))),
        "cident": ident, "cmask": maskbig, "cones6": ones6, "cramp3": ramp3,
    }
    in_maps = []
    for c in range(NCORES):
        m = dict(common)
        m["x"] = np.ascontiguousarray(x[S * c:S * (c + 1)])
        in_maps.append(m)
    return in_maps


def kernel(L0_attn_x, L0_attn_W_attn, L0_attn_b_attn, L0_attn_W_proj,
           L0_attn_b_proj, L0_attn_temperature, L0_ffn_ternary_weight,
           L0_ffn_bias, L0_ffn_act_scale):
    if "nc" not in _CACHE:
        _CACHE["nc"] = build_program()
    nc = _CACHE["nc"]
    x = np.ascontiguousarray(np.asarray(L0_attn_x, dtype=np.float32))
    wa = np.ascontiguousarray(np.asarray(L0_attn_W_attn, dtype=np.float32))
    ba = np.ascontiguousarray(np.asarray(L0_attn_b_attn, dtype=np.float32))
    wp = np.ascontiguousarray(np.asarray(L0_attn_W_proj, dtype=np.float32))
    bp = np.ascontiguousarray(np.asarray(L0_attn_b_proj, dtype=np.float32))
    tmp = np.asarray(L0_attn_temperature, dtype=np.float32).reshape(1)
    tw = np.ascontiguousarray(np.asarray(L0_ffn_ternary_weight, dtype=np.int32))
    bff = np.ascontiguousarray(np.asarray(L0_ffn_bias, dtype=np.float32))
    asc = np.asarray(L0_ffn_act_scale, dtype=np.float32).reshape(1)
    in_maps = _prep_inputs(x, wa, ba, wp, bp, tmp, tw, bff, asc)
    res = run_bass_kernel_spmd(nc, in_maps, core_ids=list(range(NCORES)))
    out = np.concatenate([r["y"] for r in res.results], axis=0)
    return out.reshape(B, T, C)


# version 11 (final)
# speedup vs baseline: 1.0161x; 1.0161x over previous
import math
import numpy as np
import ml_dtypes

import concourse.bass as bass
import concourse.mybir as mybir
from concourse.bass_utils import run_bass_kernel_spmd

F32 = mybir.dt.float32
BF16 = mybir.dt.bfloat16
I32 = mybir.dt.int32
AF = mybir.ActivationFunctionType
ALU = mybir.AluOpType
AX = mybir.AxisListType
BF = ml_dtypes.bfloat16

NCORES = 8
B = 16
S = 2
T = 2048
C = 1024
H = 16
D = 64
NCH = C // 128
NT = T // 128
NTB = T // 512
NP = H // 2
MAGIC = float(1 << 23)
NEGBIG = -1.0e30
WARMUP = 512

_CACHE = {}


def _host_consts():
    ident = np.eye(128, dtype=BF)
    qq = np.arange(128)[:, None]
    kk = np.arange(128)[None, :]
    maskbig = np.where(kk <= qq, 0.0, NEGBIG).astype(BF)
    ones6 = np.zeros((128, 128), np.float32)
    ones6[0:3, :] = 1.0
    ones6[64:67, :] = 1.0
    ones6 = ones6.astype(BF)
    start = 2.0 ** (-(2.0 ** (-(math.log2(H) - 3))))
    slopes = np.array([start * start**i for i in range(H)], np.float64)
    k = np.arange(T, dtype=np.float64)
    ramp3 = np.zeros((H, 3, T), BF)
    for h in range(H):
        v = slopes[h] * k
        hi = v.astype(BF)
        mid = (v - hi.astype(np.float64)).astype(BF)
        lo = (v - hi.astype(np.float64) - mid.astype(np.float64)).astype(BF)
        ramp3[h, 0], ramp3[h, 1], ramp3[h, 2] = hi, mid, lo
    return ident, maskbig, ones6, ramp3


class Ch:
    def __init__(self, nc, stack, name):
        self.sem = stack.enter_context(nc.semaphore(name))
        self.n = 0


class Sched:
    ENGS = ("tensor", "vector", "scalar", "sync", "gpsimd")

    def __init__(self):
        self.ops = {k: [] for k in self.ENGS}

    def do(self, eng, fn, inc=None, k=1):
        if inc is not None:
            inc.n += k
            self.ops[eng].append(
                lambda e, f=fn, s=inc.sem, kk=k: f(e).then_inc(s, kk))
        else:
            self.ops[eng].append(fn)

    def nop(self, eng):
        self.ops[eng].append(lambda e: e.nop())

    def wait(self, eng, ch, n=None):
        tgt = ch.n if n is None else n
        if tgt > 0:
            self.ops[eng].append(lambda e, s=ch.sem, t=tgt: e.wait_ge(s, t))

    def emit(self, nc, name=None):
        with nc.Block(name) as blk:
            for eng in self.ENGS:
                ops = self.ops[eng]
                if not ops:
                    continue
                getattr(blk, eng)(lambda e, o=tuple(ops): [f(e) for f in o])


class DStream:

    def __init__(self, ch, name):
        self.c = [ch(name + "a"), ch(name + "b")]

    def do(self, sc, eng, fn, idx):
        sc.do(eng, fn, inc=self.c[idx % 2], k=16)

    def wait_done(self, sc, eng, idx):
        if idx < 0:
            return
        sc.wait(eng, self.c[idx % 2], 16 * (idx // 2 + 1))

    def wait_all(self, sc, eng):
        sc.wait(eng, self.c[0])
        sc.wait(eng, self.c[1])


def r3(ap, b, f=128):
    return ap.rearrange("p (b f) -> p b f", b=b, f=f)


def build_program(dbg=(), detect_races=False):
    nc = bass.Bass("TRN2", target_bir_lowering=False,
                   detect_race_conditions=detect_races)

    x_in = nc.dram_tensor("x", [S, T, C], F32, kind="ExternalInput").ap()
    wa_in = nc.dram_tensor("wa", [C, 3 * C], F32, kind="ExternalInput").ap()
    wp_in = nc.dram_tensor("wp", [C, C], F32, kind="ExternalInput").ap()
    tw_in = nc.dram_tensor("tw", [C, C], I32, kind="ExternalInput").ap()
    ba24_in = nc.dram_tensor("ba24", [128, 24], F32, kind="ExternalInput").ap()
    bp8_in = nc.dram_tensor("bp8", [128, 8], F32, kind="ExternalInput").ap()
    bf8_in = nc.dram_tensor("bf8", [128, 8], F32, kind="ExternalInput").ap()
    bvr_in = nc.dram_tensor("bvrep", [128, C], F32, kind="ExternalInput").ap()
    sc_in = nc.dram_tensor("scals", [128, 2], F32, kind="ExternalInput").ap()
    id_in = nc.dram_tensor("cident", [128, 128], BF16, kind="ExternalInput").ap()
    mk_in = nc.dram_tensor("cmask", [128, 128], BF16, kind="ExternalInput").ap()
    o6_in = nc.dram_tensor("cones6", [128, 128], BF16, kind="ExternalInput").ap()
    rp_in = nc.dram_tensor("cramp3", [H, 3, T], BF16, kind="ExternalInput").ap()

    y_out = nc.dram_tensor("y", [S, T, C], F32, kind="ExternalOutput").ap()
    dbg_t = {}
    for name, shape, dt in (
        ("dQT", [128, NCH * T], BF16), ("dKT", [128, NCH * T], BF16),
        ("dVT", [128, NT * C], BF16), ("dHT", [128, NCH * T], BF16),
        ("dPR", [128, T], BF16), ("dX2", [128, 4 * C], F32),
        ("dWB", [128, NCH * 1536], BF16), ("dSC", [128, 24], F32),
        ("dXS", [128, 2 * T], F32), ("dD8", [128, 24], F32),
        ("dPB", [128, 4 * T], BF16), ("dPT", [128, NT * 512], BF16),
    ):
        if name in dbg:
            dbg_t[name] = nc.dram_tensor(name, shape, dt, kind="ExternalOutput").ap()

    ytk = "ExternalOutput" if "dYT" in dbg else "Internal"
    yt_dram = nc.dram_tensor("dYT", [NCH, 128, T], BF16, kind=ytk).ap()
    wp_dram = nc.dram_tensor("wpd", [C, C], BF16, kind="Internal").ap()
    wf_dram = nc.dram_tensor("wfd", [C, C], BF16, kind="Internal").ap()

    import contextlib
    with contextlib.ExitStack() as ST:
        E = ST.enter_context
        ident = E(nc.sbuf_tensor("identt", [128, 128], BF16))
        maskb = E(nc.sbuf_tensor("maskbt", [128, 128], BF16))
        ones6 = E(nc.sbuf_tensor("ones6t", [128, 128], BF16))
        ba24 = E(nc.sbuf_tensor("ba24t", [128, 24], F32))
        bp8 = E(nc.sbuf_tensor("bp8t", [128, 8], F32))
        bf8 = E(nc.sbuf_tensor("bf8t", [128, 8], F32))
        bqc8 = E(nc.sbuf_tensor("bqc8", [128, 8], F32))
        sc11 = E(nc.sbuf_tensor("sc11", [128, 8], F32))
        cscale = E(nc.sbuf_tensor("cscale", [128, 1], F32))
        epsb = E(nc.sbuf_tensor("epsb", [128, 1], F32))
        rsb = E(nc.sbuf_tensor("rsb", [128, 1], F32))
        ssb = E(nc.sbuf_tensor("ssb", [128, 1], F32))
        bvb = E(nc.sbuf_tensor("bvb", [128, C], F32))

        chans = {}

        def ch(name):
            if name not in chans:
                chans[name] = Ch(nc, ST, name)
            return chans[name]

        _phase_0(nc, ch, sc_in, id_in, mk_in, o6_in, ba24_in, bp8_in, bf8_in,
                 bvr_in, wp_in, tw_in, wp_dram, wf_dram, ident, maskb, ones6,
                 ba24, bp8, bf8, bqc8, sc11, cscale, rsb, ssb, bvb, epsb)
        for s in range(S):
            with (
                nc.sbuf_tensor(f"QT{s}", [128, NCH * T], BF16) as QT,
                nc.sbuf_tensor(f"KT{s}", [128, NCH * T], BF16) as KT,
                nc.sbuf_tensor(f"VT{s}", [128, NT * C], BF16) as VT,
            ):
                _phase_AB(nc, ch, s, x_in, wa_in, QT, KT, VT, ba24, bqc8,
                          cscale, bvb, epsb, dbg_t, sc11, ssb, rsb)
                _phase_C(nc, ch, s, QT, KT, VT, ident, maskb, ones6, rp_in,
                         yt_dram, dbg_t, epsb)
            _phase_D(nc, ch, s, x_in, y_out, yt_dram, wp_dram, wf_dram,
                     bp8, bf8, rsb, ssb, epsb, y_final=(s == S - 1), dbg_t=dbg_t)

    return nc


def _phase_0(nc, ch, sc_in, id_in, mk_in, o6_in, ba24_in, bp8_in, bf8_in,
             bvr_in, wp_in, tw_in, wp_dram, wf_dram, ident, maskb, ones6,
             ba24, bp8, bf8, bqc8, sc11, cscale, rsb, ssb, bvb, epsb):
    with (
        nc.sbuf_tensor("wstg_f", [128, 2 * C], F32) as wstg_f,
        nc.sbuf_tensor("wstg_i", [128, 2 * C], I32) as wstg_i,
        nc.sbuf_tensor("wstg_b", [128, 2 * C], BF16) as wstg_b,
        nc.sbuf_tensor("wstg_t", [128, 2 * C], BF16) as wstg_t,
        nc.sbuf_tensor("warm", [128, WARMUP], F32) as warm,
    ):
        sc = Sched()
        c_ld = ch("c_ld")
        sc.do("gpsimd", lambda e: e.memset(epsb[:, :], 1e-5), inc=ch("c_eps"))
        sc.wait("scalar", ch("c_eps"))
        for i in range(WARMUP):
            sc.do("scalar", lambda e, i=i: e.activation(
                out=warm[:, i:i + 1], in_=epsb[:, 0:1], func=AF.Exp,
                bias=epsb[:, 0:1], scale=1.0))
        for dst, src in ((ident, id_in), (maskb, mk_in), (ones6, o6_in),
                         (ba24, ba24_in), (bp8, bp8_in), (bf8, bf8_in),
                         (bvb, bvr_in)):
            sc.do("gpsimd", lambda e, d=dst, s=src: e.dma_start(
                out=d[:, :], in_=s[:, :]), inc=c_ld, k=16)
        sc.do("gpsimd", lambda e: e.dma_start(
            out=sc11[:, 0:2], in_=sc_in[:, :]), inc=c_ld, k=16)
        c_sc = ch("c_sc")
        c_a0 = ch("c_a0")
        sc.wait("vector", c_ld)
        sc.do("vector", lambda e: e.tensor_scalar(
            out=sc11[:, 2:3], in0=sc11[:, 0:1], scalar1=8.0, scalar2=None,
            op0=ALU.mult))
        sc.do("vector", lambda e: e.tensor_scalar(
            out=ssb[:, 0:1], in0=sc11[:, 1:2], scalar1=1e-5, scalar2=None,
            op0=ALU.max))
        sc.do("vector", lambda e: e.tensor_tensor(
            out=sc11[:, 3:4], in0=sc11[:, 2:3], in1=sc11[:, 2:3], op=ALU.mult))
        sc.do("vector", lambda e: e.tensor_scalar(
            out=sc11[:, 4:5], in0=sc11[:, 2:3], scalar1=1e38, scalar2=1.0,
            op0=ALU.mult, op1=ALU.min))
        sc.nop("vector")
        sc.do("vector", lambda e: e.tensor_scalar(
            out=sc11[:, 4:5], in0=sc11[:, 4:5], scalar1=-1.0, scalar2=None,
            op0=ALU.max), inc=c_sc)
        sc.wait("scalar", c_sc)
        sc.do("scalar", lambda e: e.activation(
            out=sc11[:, 5:6], in_=sc11[:, 3:4], func=AF.Ln))
        sc.do("scalar", lambda e: e.activation(
            out=sc11[:, 6:7], in_=ssb[:, 0:1], func=AF.Ln))
        sc.do("scalar", lambda e: e.activation(
            out=sc11[:, 5:6], in_=sc11[:, 5:6], func=AF.Exp, scale=-0.5))
        sc.do("scalar", lambda e: e.activation(
            out=rsb[:, 0:1], in_=sc11[:, 6:7], func=AF.Exp, scale=-1.0),
            inc=c_a0)
        c_bqc = ch("c_bqc")
        sc.wait("vector", c_a0)
        sc.do("vector", lambda e: e.tensor_tensor(
            out=cscale[:, 0:1], in0=sc11[:, 5:6], in1=sc11[:, 4:5], op=ALU.mult))
        sc.nop("vector")
        sc.do("vector", lambda e: e.tensor_scalar(
            out=bqc8[:, :], in0=ba24[:, 0:8], scalar1=cscale[:, 0:1], scalar2=None,
            op0=ALU.mult), inc=c_bqc)

        c_l = ch("c_l")
        c_wc = ch("c_wc")
        c_wpw = ch("c_wpw")
        c_wt = ch("c_wt")
        c_wfw = ch("c_wfw")
        for cc in range(NCH):
            pg = (cc % 2) * C
            sc.do("gpsimd", lambda e, cc=cc, pg=pg: e.dma_start(
                out=wstg_f[:, pg:pg + C], in_=wp_in[cc * 128:(cc + 1) * 128, :]),
                inc=c_l, k=16)
            sc.wait("vector", c_l)
            if cc >= 2:
                sc.wait("vector", c_wpw, 16 * (cc - 1))
            sc.do("vector", lambda e, pg=pg: e.tensor_copy(
                wstg_b[:, pg:pg + C], wstg_f[:, pg:pg + C]), inc=c_wc)
            sc.wait("gpsimd", c_wc)
            sc.do("gpsimd", lambda e, cc=cc, pg=pg: e.dma_start(
                out=wp_dram[cc * 128:(cc + 1) * 128, :], in_=wstg_b[:, pg:pg + C]),
                inc=c_wpw, k=16)
        for oc in range(NCH):
            pg = (oc % 2) * C
            sc.do("gpsimd", lambda e, oc=oc, pg=pg: e.dma_start(
                out=wstg_i[:, pg:pg + C], in_=tw_in[oc * 128:(oc + 1) * 128, :]),
                inc=c_l, k=16)
            sc.wait("vector", c_l)
            sc.wait("vector", c_wpw)
            if oc >= 2:
                sc.wait("vector", c_wt, 16 * (oc - 1))
            sc.do("vector", lambda e, pg=pg: e.tensor_scalar(
                out=wstg_b[:, pg:pg + C], in0=wstg_i[:, pg:pg + C],
                scalar1=1.0, scalar2=None, op0=ALU.subtract), inc=c_wc)
            sc.wait("sync", c_wc)
            if oc >= 1:
                sc.wait("sync", c_wfw, 16 * 8 * oc)
            sc.do("sync", lambda e, pg=pg: e.dma_start_transpose(
                r3(wstg_t[:, pg:pg + C], 8), r3(wstg_b[:, pg:pg + C], 8)),
                inc=c_wt, k=16)
            sc.wait("gpsimd", c_wt)
            for ic in range(NCH):
                sc.do("gpsimd", lambda e, oc=oc, ic=ic, pg=pg: e.dma_start(
                    out=wf_dram[ic * 128:(ic + 1) * 128, oc * 128:(oc + 1) * 128],
                    in_=wstg_t[:, pg + ic * 128:pg + (ic + 1) * 128]),
                    inc=c_wfw, k=16)
        sc.emit(nc, "ph0")


def _phase_AB(nc, ch, s, x_in, wa_in, QT, KT, VT, ba24, bqc8, cscale, bvb,
              epsb, dbg_t, sc11=None, ssb=None, rsb=None):
    with (
        nc.sbuf_tensor(f"hT{s}", [128, NT * C], BF16) as hT,
        nc.sbuf_tensor(f"xa{s}", [128, 2 * C], F32) as xa,
        nc.sbuf_tensor(f"hb{s}", [128, 2 * C], BF16) as hb,
        nc.sbuf_tensor(f"sqscr{s}", [128, C], BF16) as sqscr,
        nc.sbuf_tensor(f"lnst{s}", [128, 128], F32) as lnst,
        nc.sbuf_tensor(f"wab{s}", [128, NCH * 1536], BF16) as wab,
        nc.sbuf_tensor(f"wstg{s}", [128, 2 * 1536], F32) as wstg,
        nc.psum_tensor(f"pq{s}", [128, 1024], F32) as pq,
    ):
        sc = Sched()
        axs = DStream(ch, f"ax{s}")
        hts = DStream(ch, f"ht{s}")
        c_st = ch(f"c_st{s}")
        c_sq = ch(f"c_sq{s}")
        c_ap = ch(f"c_ap{s}")
        for tt in range(NT):
            lb = tt * 8
            pg = (tt % 2) * C
            if tt >= 2:
                sc.wait("sync", c_ap, tt - 1)
            axs.do(sc, "sync", lambda e, tt=tt, pg=pg: e.dma_start(
                out=xa[:, pg:pg + C], in_=x_in[s, tt * 128:(tt + 1) * 128, :]), tt)
            axs.wait_done(sc, "vector", tt)
            sc.do("vector", lambda e, pg=pg, lb=lb: e.reduce_sum(
                lnst[:, lb:lb + 1], xa[:, pg:pg + C], axis=AX.X))
            axs.wait_done(sc, "scalar", tt)
            sc.do("scalar", lambda e, pg=pg, lb=lb: e.activation(
                out=sqscr[:, :], in_=xa[:, pg:pg + C], func=AF.Square,
                accum_out=lnst[:, lb + 1:lb + 2]), inc=c_sq)
            sc.wait("vector", c_sq)
            sc.do("vector", lambda e, lb=lb: e.tensor_scalar(
                out=lnst[:, lb + 2:lb + 3], in0=lnst[:, lb:lb + 1],
                scalar1=1.0 / C, scalar2=None, op0=ALU.mult))
            sc.do("vector", lambda e, lb=lb: e.tensor_scalar(
                out=lnst[:, lb + 3:lb + 4], in0=lnst[:, lb + 1:lb + 2],
                scalar1=1.0 / C, scalar2=None, op0=ALU.mult))
            sc.do("vector", lambda e, lb=lb: e.tensor_tensor(
                out=lnst[:, lb + 4:lb + 5], in0=lnst[:, lb + 2:lb + 3],
                in1=lnst[:, lb + 2:lb + 3], op=ALU.mult))
            sc.nop("vector")
            sc.do("vector", lambda e, lb=lb: e.tensor_tensor(
                out=lnst[:, lb + 5:lb + 6], in0=lnst[:, lb + 3:lb + 4],
                in1=lnst[:, lb + 4:lb + 5], op=ALU.subtract), inc=c_st)
            sc.wait("scalar", c_st)
            sc.do("scalar", lambda e, lb=lb: e.activation(
                out=lnst[:, lb + 6:lb + 7], in_=lnst[:, lb + 5:lb + 6],
                func=AF.Ln, bias=epsb[:, 0:1], scale=1.0))
            sc.nop("scalar")
            sc.do("scalar", lambda e, lb=lb: e.activation(
                out=lnst[:, lb + 6:lb + 7], in_=lnst[:, lb + 6:lb + 7],
                func=AF.Exp, scale=-0.5), inc=c_sq)
            sc.wait("vector", c_sq)
            sc.do("vector", lambda e, lb=lb: e.scalar_tensor_tensor(
                out=lnst[:, lb + 7:lb + 8], in0=lnst[:, lb + 2:lb + 3],
                scalar=-1.0, in1=lnst[:, lb + 6:lb + 7],
                op0=ALU.mult, op1=ALU.mult), inc=c_st)
            sc.wait("scalar", c_st)
            hts.wait_done(sc, "scalar", tt - 2)
            sc.do("scalar", lambda e, pg=pg, lb=lb: e.activation(
                out=hb[:, pg:pg + C], in_=xa[:, pg:pg + C], func=AF.Identity,
                bias=lnst[:, lb + 7:lb + 8], scale=lnst[:, lb + 6:lb + 7]),
                inc=c_ap)
            sc.wait("sync", c_ap)
            hts.do(sc, "sync", lambda e, tt=tt, pg=pg: e.dma_start_transpose(
                r3(hT[:, tt * C:(tt + 1) * C], NCH),
                r3(hb[:, pg:pg + C], NCH)), tt)

        wss = DStream(ch, f"ws{s}")
        c_wc = ch(f"c_wc{s}")
        c_mm = ch(f"c_mm{s}")
        c_ev = ch(f"c_ev{s}")
        nmm = [0]
        nev = [0]
        hTv = hT[:, :].rearrange("p (t k) -> p t k", t=NT, k=C)
        for half in range(2):
            hoff = half * 1536
            for cc in range(NCH):
                gidx = half * NCH + cc
                pg = (gidx % 2) * 1536
                if gidx >= 2:
                    sc.wait("sync", c_wc, gidx - 1)
                if half == 1 and cc == 0:
                    sc.wait("vector", c_mm, 48)
                wss.do(sc, "sync", lambda e, cc=cc, pg=pg, hoff=hoff: e.dma_start(
                    out=wstg[:, pg:pg + 1536],
                    in_=wa_in[cc * 128:(cc + 1) * 128, hoff:hoff + 1536]), gidx)
                wss.wait_done(sc, "vector", gidx)
                sc.do("vector", lambda e, cc=cc, pg=pg: e.tensor_copy(
                    wab[:, cc * 1536:cc * 1536 + 1536], wstg[:, pg:pg + 1536]),
                    inc=c_wc)
            sc.wait("tensor", c_wc)
            hts.wait_all(sc, "tensor")
            for ocl in range(12):
                oc = half * 12 + ocl
                if oc >= 16:
                    break
                for tb in range(NTB):
                    bank = (nmm[0] % 2) * 512
                    if nmm[0] >= 2:
                        sc.wait("tensor", c_ev, nev[0] - 1)
                    for cc in range(NCH):
                        sc.do("tensor", lambda e, cc=cc, ocl=ocl, tb=tb, bank=bank:
                              e.matmul(
                                  pq[:, bank:bank + 512],
                                  wab[:, cc * 1536 + ocl * 128:cc * 1536 + ocl * 128 + 128],
                                  hTv[:, 4 * tb:4 * tb + 4, cc * 128:(cc + 1) * 128],
                                  start=(cc == 0), stop=(cc == NCH - 1)),
                              inc=(c_mm if cc == NCH - 1 else None))
                    nmm[0] += 1
                    sc.wait("scalar", c_mm)
                    if oc < 8:
                        sc.do("scalar", lambda e, oc=oc, tb=tb, bank=bank:
                              e.activation(
                                  out=QT[:, oc * T + tb * 512:oc * T + tb * 512 + 512],
                                  in_=pq[:, bank:bank + 512], func=AF.Identity,
                                  bias=bqc8[:, oc:oc + 1], scale=cscale[:, 0:1]),
                              inc=c_ev)
                    else:
                        sc.do("scalar", lambda e, oc=oc, tb=tb, bank=bank:
                              e.activation(
                                  out=KT[:, (oc - 8) * T + tb * 512:(oc - 8) * T + tb * 512 + 512],
                                  in_=pq[:, bank:bank + 512], func=AF.Identity,
                                  bias=ba24[:, oc:oc + 1], scale=1.0), inc=c_ev)
                    nev[0] += 1
        for tt in range(NT):
            for vb in range(2):
                bank = (nmm[0] % 2) * 512
                sc.wait("tensor", c_ev, nev[0] - 1)
                for cc in range(NCH):
                    sc.do("tensor", lambda e, cc=cc, tt=tt, vb=vb, bank=bank:
                          e.matmul(
                              pq[:, bank:bank + 512],
                              hT[:, tt * C + cc * 128:tt * C + cc * 128 + 128],
                              wab[:, cc * 1536 + 512 + vb * 512:cc * 1536 + 512 + vb * 512 + 512],
                              start=(cc == 0), stop=(cc == NCH - 1)),
                          inc=(c_mm if cc == NCH - 1 else None))
                nmm[0] += 1
                sc.wait("vector", c_mm)
                sc.do("vector", lambda e, tt=tt, vb=vb, bank=bank: e.tensor_tensor(
                    out=VT[:, tt * C + vb * 512:tt * C + vb * 512 + 512],
                    in0=pq[:, bank:bank + 512], in1=bvb[:, vb * 512:vb * 512 + 512],
                    op=ALU.add), inc=c_ev)
                nev[0] += 1

        if "dSC" in dbg_t and s == 0:
            sc.wait("gpsimd", c_ev, 1)
            sc.do("gpsimd", lambda e: e.dma_start(
                out=dbg_t["dSC"][:, 0:8], in_=sc11[:, :],
                allow_slow_non_contiguous=True), inc=ch("c_dbg"), k=16)
            sc.do("gpsimd", lambda e: e.dma_start(
                out=dbg_t["dSC"][:, 8:9], in_=cscale[:, :],
                allow_slow_non_contiguous=True), inc=ch("c_dbg"), k=16)
            sc.do("gpsimd", lambda e: e.dma_start(
                out=dbg_t["dSC"][:, 9:10], in_=ssb[:, :],
                allow_slow_non_contiguous=True), inc=ch("c_dbg"), k=16)
            sc.do("gpsimd", lambda e: e.dma_start(
                out=dbg_t["dSC"][:, 10:11], in_=rsb[:, :],
                allow_slow_non_contiguous=True), inc=ch("c_dbg"), k=16)
            sc.do("gpsimd", lambda e: e.dma_start(
                out=dbg_t["dSC"][:, 11:19], in_=bqc8[:, :],
                allow_slow_non_contiguous=True), inc=ch("c_dbg"), k=16)
        if "dHT" in dbg_t and s == 0:
            hts.wait_all(sc, "gpsimd")
            sc.do("gpsimd", lambda e: e.dma_start(out=dbg_t["dHT"][:, :], in_=hT[:, :]),
                  inc=ch("c_dbg"), k=16)
        if "dWB" in dbg_t and s == 0:
            sc.wait("gpsimd", c_wc)
            sc.do("gpsimd", lambda e: e.dma_start(out=dbg_t["dWB"][:, :], in_=wab[:, :]),
                  inc=ch("c_dbg"), k=16)
        if ("dQT" in dbg_t or "dVT" in dbg_t) and s == 0:
            sc.wait("gpsimd", c_ev)
            for nm, t in (("dQT", QT), ("dKT", KT), ("dVT", VT)):
                if nm in dbg_t:
                    sc.do("gpsimd", lambda e, nm=nm, t=t: e.dma_start(
                        out=dbg_t[nm][:, :], in_=t[:, :]), inc=ch("c_dbg"), k=16)
        sc.emit(nc, f"phAB{s}")


def _phase_C(nc, ch, s, QT, KT, VT, ident, maskb, ones6, rp_in, yt_dram,
             dbg_t, epsb=None):
    with (
        nc.sbuf_tensor(f"rampt{s}", [67, 2 * T], BF16) as rampt,
        nc.sbuf_tensor(f"Xs{s}", [128, 2 * T], F32) as Xs,
        nc.sbuf_tensor(f"Pb0{s}", [128, 4 * T], BF16) as Pb0,
        nc.sbuf_tensor(f"Pb1{s}", [128, 4 * T], BF16) as Pb1,
        nc.sbuf_tensor(f"Pn{s}", [128, 2 * T], BF16) as Pn,
        nc.sbuf_tensor(f"PT0{s}", [128, NT * 512], BF16) as PT0,
        nc.sbuf_tensor(f"PT1{s}", [128, NT * 512], BF16) as PT1,
        nc.sbuf_tensor(f"negp{s}", [128, 8], F32) as negp,
        nc.sbuf_tensor(f"den8{s}", [128, 8], F32) as den8,
        nc.sbuf_tensor(f"r8{s}", [128, 8], F32) as r8,
        nc.sbuf_tensor(f"yts{s}", [128, 2 * 512], BF16) as yts,
        nc.psum_tensor(f"SH0{s}", [128, 1536], F32) as SH0,
        nc.psum_tensor(f"SH1{s}", [128, 1536], F32) as SH1,
        nc.psum_tensor(f"AVB{s}", [128, 1024], F32) as AVB,
    ):
        sc = Sched()
        c_rl = ch(f"c_rl{s}")
        c_sm = ch(f"c_sm{s}")
        c_se = ch(f"c_se{s}")
        c_xr = ch(f"c_xr{s}")
        c_ex = ch(f"c_ex{s}")
        c_rr = ch(f"c_rr{s}")
        c_pn = ch(f"c_pn{s}")
        pns = DStream(ch, f"pt{s}")
        c_am = ch(f"c_am{s}")
        c_ye = ch(f"c_ye{s}")
        yss = DStream(ch, f"ys{s}")
        Pb = (Pb0, Pb1)
        ptflat = (PT0, PT1)
        ptview = [PT0[:, :].rearrange("p (qs k) -> p qs k", qs=4, k=NT * 128),
                  PT1[:, :].rearrange("p (qs k) -> p qs k", qs=4, k=NT * 128)]
        xbuf_last_exp = [0, 0]
        se_last = {0: 0, 1: 0}
        gidx = [0]
        nyts = [0]
        npn = [0]

        for p in range(NP):
            rpg = (p % 2) * T
            if p >= 2:
                sc.wait("gpsimd", c_sm, c_sm.n)
            sc.do("gpsimd", lambda e, p=p, rpg=rpg: e.dma_start(
                out=rampt[0:3, rpg:rpg + T], in_=rp_in[2 * p, :, :]), inc=c_rl, k=16)
            sc.do("gpsimd", lambda e, p=p, rpg=rpg: e.dma_start(
                out=rampt[64:67, rpg:rpg + T], in_=rp_in[2 * p + 1, :, :]),
                inc=c_rl, k=16)
            sc.wait("tensor", c_rl)
            for qb in range(NTB):
                g = p * NTB + qb
                for qsub in range(4):
                    qt = qb * 4 + qsub
                    ktot = (qt + 1) * 128
                    qoff = qt * 128
                    chunks = [(0, min(ktot, 1536))]
                    if ktot > 1536:
                        chunks.append((1536, ktot - 1536))
                    for h in range(2):
                        SH = SH0 if h == 0 else SH1
                        row0 = 0 if h == 0 else 64
                        tp = (row0, 0)
                        Qst = QT[row0:row0 + 64, p * T + qoff:p * T + qoff + 128]
                        for ci, (c0, w) in enumerate(chunks):
                            sc.wait("tensor", c_se, se_last[h])
                            kbs = [(c0 + i * 512, min(512, w - i * 512))
                                   for i in range((w + 511) // 512)]
                            last_kb = len(kbs) - 1
                            has_diag = (c0 + w == ktot)
                            for i, (k0, kw) in enumerate(kbs):
                                sc.do("tensor", lambda e, SH=SH, c0=c0, k0=k0,
                                      kw=kw, Qst=Qst, row0=row0, p=p, tp=tp:
                                      e.matmul(
                                          SH[:, k0 - c0:k0 - c0 + kw], Qst,
                                          KT[row0:row0 + 64,
                                             p * T + k0:p * T + k0 + kw],
                                          start=True, stop=False,
                                          tile_position=tp))
                            for i, (k0, kw) in enumerate(kbs):
                                isdiagkb = has_diag and i == last_kb
                                stop = not isdiagkb
                                fin = (i == last_kb) and not has_diag
                                sc.do("tensor", lambda e, SH=SH, c0=c0, k0=k0,
                                      kw=kw, row0=row0, rpg=rpg, stop=stop, tp=tp:
                                      e.matmul(
                                          SH[:, k0 - c0:k0 - c0 + kw],
                                          ones6[row0:row0 + 3, :],
                                          rampt[row0:row0 + 3,
                                                rpg + k0:rpg + k0 + kw],
                                          start=False, stop=stop,
                                          tile_position=tp),
                                      inc=(c_sm if fin else None))
                            if has_diag:
                                sc.do("tensor", lambda e, SH=SH, c0=c0, qoff=qoff:
                                      e.matmul(
                                          SH[:, qoff - c0:qoff - c0 + 128],
                                          ident[:, :], maskb[:, :],
                                          start=False, stop=True), inc=c_sm)
                            b = gidx[0] % 2
                            sc.wait("vector", c_sm)
                            if ci == 0:
                                sc.wait("vector", c_ex, xbuf_last_exp[b])
                            last_ev = (ci == len(chunks) - 1)
                            sc.do("vector", lambda e, SH=SH, b=b, c0=c0, w=w,
                                  h=h, ci=ci: e.tensor_scalar(
                                      out=Xs[:, b * T + c0:b * T + c0 + w],
                                      in0=SH[:, 0:w], scalar1=-1.0, scalar2=None,
                                      op0=ALU.mult, op1=ALU.min,
                                      accum_out=negp[:, h * 4 + ci:h * 4 + ci + 1]),
                                  inc=c_se)
                            se_last[h] = c_se.n
                        nch_ = len(chunks)
                        if nch_ == 1:
                            negm_ap = negp[:, h * 4:h * 4 + 1]
                            sc.do("vector", lambda e, h=h: e.tensor_scalar(
                                out=negp[:, h * 4 + 2:h * 4 + 3],
                                in0=epsb[:, 0:1], scalar1=1.0,
                                scalar2=None, op0=ALU.mult), inc=c_xr)
                        else:
                            negm_ap = negp[:, h * 4 + 3:h * 4 + 4]
                            sc.do("vector", lambda e, h=h: e.tensor_scalar(
                                out=negp[:, h * 4 + 2:h * 4 + 3],
                                in0=epsb[:, 0:1], scalar1=1.0,
                                scalar2=None, op0=ALU.mult))
                            sc.do("vector", lambda e, h=h, nch_=nch_:
                                  e.tensor_reduce(
                                      negp[:, h * 4 + 3:h * 4 + 4],
                                      negp[:, h * 4:h * 4 + nch_],
                                      axis=AX.X, op=ALU.min), inc=c_xr)
                        b = gidx[0] % 2
                        sc.wait("scalar", c_xr)
                        sc.wait("scalar", c_pn, 8 * g)
                        sc.do("scalar", lambda e, b=b, h=h, qsub=qsub, ktot=ktot,
                              negm_ap=negm_ap: e.activation(
                                  out=Pb[h][:, qsub * T:qsub * T + ktot],
                                  in_=Xs[:, b * T:b * T + ktot], func=AF.Exp,
                                  bias=negm_ap, scale=-1.0,
                                  accum_out=den8[:, h * 4 + qsub:h * 4 + qsub + 1]),
                              inc=c_ex)
                        xbuf_last_exp[b] = c_ex.n
                        gidx[0] += 1
                        if ("dXS" in dbg_t and s == 0 and p == 0 and qb == 0
                                and qsub == 1 and h == 0):
                            cdbg = ch("c_dbg")
                            sc.wait("gpsimd", c_ex)
                            sc.do("gpsimd", lambda e: e.dma_start(
                                out=dbg_t["dXS"][:, :], in_=Xs[:, :]),
                                inc=cdbg, k=16)
                            sc.do("gpsimd", lambda e: e.dma_start(
                                out=dbg_t["dD8"][:, 0:8], in_=negp[:, :],
                                allow_slow_non_contiguous=True), inc=cdbg, k=16)
                            sc.wait("vector", cdbg)
                            sc.wait("scalar", cdbg)
                            sc.wait("tensor", cdbg)
                sc.wait("scalar", c_pn, 8 * g)
                sc.do("scalar", lambda e: e.activation(
                    out=negp[:, 2:3], in_=epsb[:, 0:1], func=AF.Identity,
                    bias=0.0, scale=1.0))
                sc.do("scalar", lambda e: e.activation(
                    out=r8[:, :], in_=den8[:, :], func=AF.Ln))
                sc.nop("scalar")
                sc.do("scalar", lambda e: e.activation(
                    out=r8[:, :], in_=r8[:, :], func=AF.Exp, scale=-1.0),
                    inc=c_rr)
                if "dPB" in dbg_t and "dD8" in dbg_t and s == 0 and p == 0 and qb == 0:
                    cdbg = ch("c_dbg")
                    sc.wait("gpsimd", c_rr)
                    sc.do("gpsimd", lambda e: e.dma_start(
                        out=dbg_t["dD8"][:, 8:16], in_=den8[:, :],
                        allow_slow_non_contiguous=True), inc=cdbg, k=16)
                    sc.do("gpsimd", lambda e: e.dma_start(
                        out=dbg_t["dD8"][:, 16:24], in_=r8[:, :],
                        allow_slow_non_contiguous=True), inc=cdbg, k=16)
                    sc.do("gpsimd", lambda e: e.dma_start(
                        out=dbg_t["dPB"][:, :], in_=Pb0[:, :]), inc=cdbg, k=16)
                    sc.wait("vector", cdbg)
                    sc.wait("scalar", cdbg)
                    sc.wait("tensor", cdbg)
                for h in range(2):
                    for qsub in range(4):
                        qt = qb * 4 + qsub
                        ktot = (qt + 1) * 128
                        i = npn[0]
                        png = (i % 2) * T
                        sc.wait("vector", c_rr)
                        pns.wait_done(sc, "vector", i - 1)
                        pns.wait_done(sc, "vector", i - 2)
                        sc.do("vector", lambda e, h=h, qsub=qsub, ktot=ktot,
                              png=png: e.tensor_scalar(
                                  out=Pn[:, png:png + ktot],
                                  in0=Pb[h][:, qsub * T:qsub * T + ktot],
                                  scalar1=r8[:, h * 4 + qsub:h * 4 + qsub + 1],
                                  scalar2=None, op0=ALU.mult), inc=c_pn)
                        eng = "sync" if (i % 2 == 0) else "scalar"
                        sc.wait(eng, c_pn)
                        sc.wait(eng, c_am, c_am.n)
                        pns.do(sc, eng, lambda e, h=h, qt=qt, qsub=qsub,
                               ktot=ktot, png=png: e.dma_start_transpose(
                                   r3(ptflat[h][:, qsub * (NT * 128):
                                                qsub * (NT * 128) + ktot], qt + 1),
                                   r3(Pn[:, png:png + ktot], qt + 1)), i)
                        npn[0] += 1
                if "dPT" in dbg_t and s == 0 and p == 0 and qb == 0:
                    cdbg = ch("c_dbg")
                    pns.wait_all(sc, "gpsimd")
                    sc.do("gpsimd", lambda e: e.dma_start(
                        out=dbg_t["dPT"][:, :], in_=PT0[:, :]), inc=cdbg, k=16)
                    sc.wait("vector", cdbg)
                    sc.wait("scalar", cdbg)
                    sc.wait("tensor", cdbg)
                    sc.wait("sync", cdbg)
                nkc = 4 * qb + 4
                pns.wait_all(sc, "tensor")
                sc.wait("tensor", c_ye, c_ye.n)
                for kc in range(nkc):
                    qsmin = max(0, kc - 4 * qb)
                    for h in range(2):
                        first = (kc == 0)
                        last = (kc == nkc - 1)
                        sc.do("tensor", lambda e, kc=kc, h=h, p=p,
                              qsmin=qsmin, first=first, last=last: e.matmul(
                                  AVB[h * 64:h * 64 + 64,
                                      h * 512 + qsmin * 128:h * 512 + 512],
                                  VT[:, kc * C + p * 128 + h * 64:
                                     kc * C + p * 128 + h * 64 + 64],
                                  ptview[h][:, qsmin:4, kc * 128:kc * 128 + 128],
                                  start=first, stop=last,
                                  tile_position=(0, h * 64)),
                              inc=(c_am if last else None))
                sc.wait("scalar", c_am)
                yss.wait_done(sc, "scalar", nyts[0] - 2)
                yb = (nyts[0] % 2) * 512
                sc.do("scalar", lambda e, yb=yb: e.copy(
                    out=yts[0:64, yb:yb + 512], in_=AVB[0:64, 0:512]))
                sc.do("scalar", lambda e, yb=yb: e.copy(
                    out=yts[64:128, yb:yb + 512], in_=AVB[64:128, 512:1024]),
                    inc=c_ye)
                sc.wait("gpsimd", c_ye)
                yss.do(sc, "gpsimd", lambda e, p=p, qb=qb, yb=yb: e.dma_start(
                    out=yt_dram[p, :, qb * 512:qb * 512 + 512],
                    in_=yts[:, yb:yb + 512]), nyts[0])
                nyts[0] += 1
            if "dPR" in dbg_t and s == 0 and p == 0:
                sc.wait("gpsimd", c_pn)
                sc.do("gpsimd", lambda e: e.dma_start(
                    out=dbg_t["dPR"][:, :], in_=Pn[:, 0:T]), inc=ch("c_dbg"), k=16)
        sc.emit(nc, f"phC{s}")


def _phase_D(nc, ch, s, x_in, y_out, yt_dram, wp_dram, wf_dram, bp8, bf8,
             rsb, ssb, epsb, y_final, dbg_t):
    with (
        nc.sbuf_tensor(f"Wp{s}", [128, NCH * C], BF16) as Wp,
        nc.sbuf_tensor(f"Wf{s}", [128, NCH * C], BF16) as Wf,
        nc.sbuf_tensor(f"ytl{s}", [128, 2 * NCH * 512], BF16) as ytl,
        nc.sbuf_tensor(f"aT{s}", [128, NCH * 512], BF16) as aT,
        nc.sbuf_tensor(f"h2T{s}", [128, NCH * 512], BF16) as h2T,
        nc.sbuf_tensor(f"atok{s}", [128, 4 * C], BF16) as atok,
        nc.sbuf_tensor(f"ftok{s}", [128, 4 * C], BF16) as ftok,
        nc.sbuf_tensor(f"xd{s}", [128, 2 * C], F32) as xd,
        nc.sbuf_tensor(f"x2{s}", [128, 4 * C], F32) as x2,
        nc.sbuf_tensor(f"h2s{s}", [128, C], F32) as h2s,
        nc.sbuf_tensor(f"h2q{s}", [128, 2 * C], BF16) as h2q,
        nc.sbuf_tensor(f"an{s}", [128, C], F32) as an,
        nc.sbuf_tensor(f"xo{s}", [128, 2 * C], F32) as xo,
        nc.sbuf_tensor(f"sqs2{s}", [128, C], BF16) as sqs2,
        nc.sbuf_tensor(f"ln2{s}", [128, 12], F32) as ln2,
        nc.psum_tensor(f"pd{s}", [128, 1024], F32) as pd,
    ):
        sc = Sched()
        c_wl = ch(f"c_wl{s}")
        for cc in range(NCH):
            sc.do("gpsimd", lambda e, cc=cc: e.dma_start(
                out=Wp[:, cc * C:(cc + 1) * C],
                in_=wp_dram[cc * 128:(cc + 1) * 128, :]), inc=c_wl, k=16)
            sc.do("gpsimd", lambda e, cc=cc: e.dma_start(
                out=Wf[:, cc * C:(cc + 1) * C],
                in_=wf_dram[cc * 128:(cc + 1) * 128, :]), inc=c_wl, k=16)
        c_yl = ch(f"c_yl{s}")
        c_dmm = ch(f"c_dmm{s}")
        c_dev = ch(f"c_dev{s}")
        c_at = ch(f"c_at{s}")
        c_au = ch(f"c_au{s}")
        c_qt = ch(f"c_qt{s}")
        xls = DStream(ch, f"xl{s}")
        c_rx = ch(f"c_rx{s}")
        c_ds = ch(f"c_ds{s}")
        c_da = ch(f"c_da{s}")
        c_r1 = ch(f"c_r1{s}")
        c_qz = ch(f"c_qz{s}")
        c_f2 = ch(f"c_f2{s}")
        c_fu = ch(f"c_fu{s}")
        sos = DStream(ch, f"so{s}")
        nmm = [0]
        atokv = atok[:, :].rearrange("p (oc k) -> p oc k", oc=NCH, k=512)
        ftokv = ftok[:, :].rearrange("p (oc k) -> p oc k", oc=NCH, k=512)

        def ln_stats(tile_ap, base, ax=AX.X):
            sc.do("vector", lambda e, t=tile_ap, bs=base, ax=ax: e.reduce_sum(
                ln2[:, bs:bs + 1], t, axis=ax))
            sc.do("scalar", lambda e, t=tile_ap, bs=base: e.activation(
                out=sqs2[:, :], in_=t, func=AF.Square,
                accum_out=ln2[:, bs + 1:bs + 2]), inc=c_da)
            sc.wait("vector", c_da)
            sc.do("vector", lambda e, bs=base: e.tensor_scalar(
                out=ln2[:, bs:bs + 1], in0=ln2[:, bs:bs + 1],
                scalar1=1.0 / C, scalar2=None, op0=ALU.mult))
            sc.do("vector", lambda e, bs=base: e.tensor_scalar(
                out=ln2[:, bs + 1:bs + 2], in0=ln2[:, bs + 1:bs + 2],
                scalar1=1.0 / C, scalar2=None, op0=ALU.mult))
            sc.do("vector", lambda e, bs=base: e.tensor_tensor(
                out=ln2[:, bs + 2:bs + 3], in0=ln2[:, bs:bs + 1],
                in1=ln2[:, bs:bs + 1], op=ALU.mult))
            sc.nop("vector")
            sc.do("vector", lambda e, bs=base: e.tensor_tensor(
                out=ln2[:, bs + 1:bs + 2], in0=ln2[:, bs + 1:bs + 2],
                in1=ln2[:, bs + 2:bs + 3], op=ALU.subtract), inc=c_ds)
            sc.wait("scalar", c_ds)
            sc.do("scalar", lambda e, bs=base: e.activation(
                out=ln2[:, bs + 1:bs + 2], in_=ln2[:, bs + 1:bs + 2],
                func=AF.Ln, bias=epsb[:, 0:1], scale=1.0))
            sc.nop("scalar")
            sc.do("scalar", lambda e, bs=base: e.activation(
                out=ln2[:, bs + 1:bs + 2], in_=ln2[:, bs + 1:bs + 2],
                func=AF.Exp, scale=-0.5), inc=c_da)
            sc.wait("vector", c_da)
            sc.do("vector", lambda e, bs=base: e.scalar_tensor_tensor(
                out=ln2[:, bs + 2:bs + 3], in0=ln2[:, bs:bs + 1], scalar=-1.0,
                in1=ln2[:, bs + 1:bs + 2], op0=ALU.mult, op1=ALU.mult), inc=c_ds)
            return ln2[:, base + 1:base + 2], ln2[:, base + 2:base + 3]

        for tb in range(NTB):
            ypg = (tb % 2) * NCH * 512
            for cc in range(NCH):
                if tb >= 2 and cc == 0:
                    sc.wait("sync", c_dmm, nmm[0] - 8)
                sc.do("sync", lambda e, cc=cc, tb=tb, ypg=ypg: e.dma_start(
                    out=ytl[:, ypg + cc * 512:ypg + cc * 512 + 512],
                    in_=yt_dram[cc, :, tb * 512:tb * 512 + 512]), inc=c_yl, k=16)
            sc.wait("tensor", c_yl)
            sc.wait("tensor", c_wl, 16 * 16)
            for oc in range(NCH):
                bank = (nmm[0] % 2) * 512
                if nmm[0] >= 2:
                    sc.wait("tensor", c_dev, c_dev.n - 1)
                for cc in range(NCH):
                    sc.do("tensor", lambda e, cc=cc, oc=oc, bank=bank, ypg=ypg:
                          e.matmul(pd[:, bank:bank + 512],
                                   Wp[:, cc * C + oc * 128:cc * C + oc * 128 + 128],
                                   ytl[:, ypg + cc * 512:ypg + cc * 512 + 512],
                                   start=(cc == 0), stop=(cc == NCH - 1)),
                          inc=(c_dmm if cc == NCH - 1 else None))
                nmm[0] += 1
                sc.wait("scalar", c_dmm)
                if tb >= 1 and oc == 0:
                    sc.wait("scalar", c_f2, 16 * 8 * tb)
                sc.do("scalar", lambda e, oc=oc, bank=bank: e.activation(
                    out=aT[:, oc * 512:oc * 512 + 512], in_=pd[:, bank:bank + 512],
                    func=AF.Identity, bias=bp8[:, oc:oc + 1], scale=1.0),
                    inc=c_dev)
            for oc in range(NCH):
                sc.wait("sync", c_dev)
                sc.wait("sync", c_au, 4 * tb)
                sc.do("sync", lambda e, oc=oc: e.dma_start_transpose(
                    r3(atok[:, oc * 512:oc * 512 + 512], 4),
                    r3(aT[:, oc * 512:oc * 512 + 512], 4)), inc=c_at, k=16)
            for tt in range(4):
                gt = tb * 4 + tt
                xpg = (tt % 2) * C
                sc.wait("sync", c_rx, gt - 1)
                xls.do(sc, "sync", lambda e, tb=tb, tt=tt, xpg=xpg: e.dma_start(
                    out=xd[:, xpg:xpg + C],
                    in_=x_in[s, tb * 512 + tt * 128:tb * 512 + tt * 128 + 128, :]),
                    gt)
                sc.wait("vector", c_at, 16 * 8 * (tb + 1))
                sc.wait("scalar", c_at, 16 * 8 * (tb + 1))
                at_ap = atokv[:, :, tt * 128:(tt + 1) * 128]
                rstd, negmr = ln_stats(at_ap, 0, ax=AX.XY)
                sc.wait("scalar", c_ds)
                sc.do("scalar", lambda e, at_ap=at_ap, r=rstd, nm=negmr:
                      e.activation(out=an[:, :], in_=at_ap, func=AF.Identity,
                                   bias=nm, scale=r), inc=c_au)
                sc.wait("vector", c_au)
                xls.wait_done(sc, "vector", gt)
                sc.do("vector", lambda e, tt=tt, xpg=xpg: e.tensor_tensor(
                    out=x2[:, tt * C:(tt + 1) * C], in0=xd[:, xpg:xpg + C],
                    in1=an[:, :], op=ALU.add), inc=c_rx)
                sc.nop("vector")
                sc.wait("scalar", c_rx)
                x2_ap = x2[:, tt * C:(tt + 1) * C]
                rstd2, negmr2 = ln_stats(x2_ap, 4)
                sc.do("vector", lambda e: e.tensor_tensor(
                    out=ln2[:, 8:9], in0=ln2[:, 5:6], in1=rsb[:, 0:1], op=ALU.mult))
                sc.do("vector", lambda e: e.tensor_tensor(
                    out=ln2[:, 9:10], in0=ln2[:, 6:7], in1=rsb[:, 0:1],
                    op=ALU.mult), inc=c_ds)
                sc.wait("scalar", c_ds)
                sc.wait("scalar", c_qz, c_qz.n)
                sc.do("scalar", lambda e, x2_ap=x2_ap: e.activation(
                    out=h2s[:, :], in_=x2_ap, func=AF.Identity,
                    bias=ln2[:, 9:10], scale=ln2[:, 8:9]), inc=c_da)
                sc.wait("vector", c_da)
                qpg = (tt % 2) * C
                sc.wait("vector", c_qt, c_qt.n)
                sc.do("vector", lambda e: e.tensor_scalar(
                    out=h2s[:, :], in0=h2s[:, :], scalar1=127.0, scalar2=MAGIC,
                    op0=ALU.min, op1=ALU.add))
                sc.nop("vector")
                sc.do("vector", lambda e, qpg=qpg: e.tensor_scalar(
                    out=h2q[:, qpg:qpg + C], in0=h2s[:, :],
                    scalar1=MAGIC - 127.0, scalar2=MAGIC,
                    op0=ALU.max, op1=ALU.subtract), inc=c_qz)
                sc.wait("scalar", c_qz)
                sc.do("scalar", lambda e, tt=tt, qpg=qpg: e.dma_start_transpose(
                    r3(h2T[:, tt * C:(tt + 1) * C], NCH),
                    r3(h2q[:, qpg:qpg + C], NCH)), inc=c_qt, k=16)
            sc.wait("tensor", c_qt)
            h2Tv = h2T[:, :].rearrange("p (t k) -> p t k", t=4, k=C)
            for oc in range(NCH):
                bank = (nmm[0] % 2) * 512
                sc.wait("tensor", c_dev, c_dev.n - 1)
                for ic in range(NCH):
                    sc.do("tensor", lambda e, ic=ic, oc=oc, bank=bank, h2Tv=h2Tv:
                          e.matmul(pd[:, bank:bank + 512],
                                   Wf[:, ic * C + oc * 128:ic * C + oc * 128 + 128],
                                   h2Tv[:, :, ic * 128:(ic + 1) * 128],
                                   start=(ic == 0), stop=(ic == NCH - 1)),
                          inc=(c_dmm if ic == NCH - 1 else None))
                nmm[0] += 1
                sc.wait("scalar", c_dmm)
                if oc == 0:
                    sc.wait("scalar", c_at, 16 * 8 * (tb + 1))
                sc.do("scalar", lambda e, oc=oc, bank=bank: e.activation(
                    out=aT[:, oc * 512:oc * 512 + 512], in_=pd[:, bank:bank + 512],
                    func=AF.Identity, bias=bf8[:, oc:oc + 1], scale=ssb[:, 0:1]),
                    inc=c_dev)
            for oc in range(NCH):
                sc.wait("sync", c_dev)
                sc.wait("sync", c_fu, 4 * tb)
                sc.do("sync", lambda e, oc=oc: e.dma_start_transpose(
                    r3(ftok[:, oc * 512:oc * 512 + 512], 4),
                    r3(aT[:, oc * 512:oc * 512 + 512], 4)), inc=c_f2, k=16)
            for tt in range(4):
                gt = tb * 4 + tt
                sc.wait("vector", c_f2, 16 * 8 * (tb + 1))
                sc.wait("scalar", c_f2, 16 * 8 * (tb + 1))
                f_ap = ftokv[:, :, tt * 128:(tt + 1) * 128]
                rstdf, negmrf = ln_stats(f_ap, 0, ax=AX.XY)
                sc.wait("scalar", c_ds)
                sc.do("scalar", lambda e, f_ap=f_ap, r=rstdf, nm=negmrf:
                      e.activation(out=an[:, :], in_=f_ap, func=AF.Identity,
                                   bias=nm, scale=r), inc=c_fu)
                sc.wait("vector", c_fu)
                opg = (tt % 2) * C
                sos.wait_done(sc, "vector", gt - 2)
                sc.do("vector", lambda e, tt=tt, opg=opg: e.tensor_tensor(
                    out=xo[:, opg:opg + C], in0=x2[:, tt * C:(tt + 1) * C],
                    in1=an[:, :], op=ALU.add), inc=c_r1)
                sc.wait("gpsimd", c_r1)
                sos.do(sc, "gpsimd", lambda e, tb=tb, tt=tt, opg=opg: e.dma_start(
                    out=y_out[s, tb * 512 + tt * 128:tb * 512 + tt * 128 + 128, :],
                    in_=xo[:, opg:opg + C]), gt)
            if "dX2" in dbg_t and s == 0 and tb == 0:
                sc.wait("gpsimd", c_rx)
                sc.do("gpsimd", lambda e: e.dma_start(
                    out=dbg_t["dX2"][:, :], in_=x2[:, :]), inc=ch("c_dbg"), k=16)
        if y_final:
            sos.wait_all(sc, "gpsimd")
            if s == 1:
                sc.wait("gpsimd", ch("so0a"))
                sc.wait("gpsimd", ch("so0b"))
        sc.emit(nc, f"phD{s}")


def _prep_inputs(x, wa, ba, wp, bp, tmp, tw, bff, asc):
    ident, maskbig, ones6, ramp3 = _host_consts()
    common = {
        "wa": wa, "wp": wp, "tw": tw,
        "ba24": np.ascontiguousarray(ba.reshape(24, 128).T),
        "bp8": np.ascontiguousarray(bp.reshape(8, 128).T),
        "bf8": np.ascontiguousarray(bff.reshape(8, 128).T),
        "bvrep": np.ascontiguousarray(
            np.broadcast_to(ba[2 * C:3 * C].reshape(1, C), (128, C))),
        "scals": np.ascontiguousarray(np.broadcast_to(
            np.array([[float(tmp[0]), float(asc[0])]], np.float32), (128, 2))),
        "cident": ident, "cmask": maskbig, "cones6": ones6, "cramp3": ramp3,
    }
    in_maps = []
    for c in range(NCORES):
        m = dict(common)
        m["x"] = np.ascontiguousarray(x[S * c:S * (c + 1)])
        in_maps.append(m)
    return in_maps


def kernel(L0_attn_x, L0_attn_W_attn, L0_attn_b_attn, L0_attn_W_proj,
           L0_attn_b_proj, L0_attn_temperature, L0_ffn_ternary_weight,
           L0_ffn_bias, L0_ffn_act_scale):
    if "nc" not in _CACHE:
        _CACHE["nc"] = build_program()
    nc = _CACHE["nc"]
    x = np.ascontiguousarray(np.asarray(L0_attn_x, dtype=np.float32))
    wa = np.ascontiguousarray(np.asarray(L0_attn_W_attn, dtype=np.float32))
    ba = np.ascontiguousarray(np.asarray(L0_attn_b_attn, dtype=np.float32))
    wp = np.ascontiguousarray(np.asarray(L0_attn_W_proj, dtype=np.float32))
    bp = np.ascontiguousarray(np.asarray(L0_attn_b_proj, dtype=np.float32))
    tmp = np.asarray(L0_attn_temperature, dtype=np.float32).reshape(1)
    tw = np.ascontiguousarray(np.asarray(L0_ffn_ternary_weight, dtype=np.int32))
    bff = np.ascontiguousarray(np.asarray(L0_ffn_bias, dtype=np.float32))
    asc = np.asarray(L0_ffn_act_scale, dtype=np.float32).reshape(1)
    in_maps = _prep_inputs(x, wa, ba, wp, bp, tmp, tw, bff, asc)
    res = run_bass_kernel_spmd(nc, in_maps, core_ids=list(range(NCORES)))
    out = np.concatenate([r["y"] for r in res.results], axis=0)
    return out.reshape(B, T, C)
